# version 02
# baseline (speedup 1.0000x reference)
import numpy as np
import ml_dtypes
from contextlib import ExitStack
import concourse.bass as bass
import concourse.mybir as mybir
from concourse.bass_utils import run_bass_kernel_spmd

F32 = mybir.dt.float32
BF16 = mybir.dt.bfloat16
AF = mybir.ActivationFunctionType
ALU = mybir.AluOpType
AX = mybir.AxisListType
BF = ml_dtypes.bfloat16

D = 1024
DK = 8
TC = 2048
HALO = 256
KV = TC + HALO
NQT = TC // 128
NKEY = 384
NE = 8
EPS = 1e-5
ATT_SCALE = 0.125
LSEQ = 4096

QTILES = [(0, 512), (512, 512), (1024, 512), (1536, 512)]
LNTILES = [(0, 256), (256, 512), (768, 512), (1280, 512), (1792, 512)]
KVTILES = [(0, 512), (512, 512), (1024, 512), (1536, 512), (2048, 256)]


class Sem:
    def __init__(self, ctx, nc, name):
        self.h = ctx.enter_context(nc.semaphore(name))
        self.n = 0

    def inc(self, instr, k=1):
        instr.then_inc(self.h, k)
        self.n += k
        return self.n

    def wait(self, eng, n=None):
        eng.wait_ge(self.h, self.n if n is None else n)


class PsumPool:
    def __init__(self, ctx, nc, sem, nbanks, tag=""):
        self.banks = [ctx.enter_context(nc.psum_tensor(f"pp{tag}{i}", [128, 512], F32))
                      for i in range(nbanks)]
        self.free_at = [0] * nbanks
        self.sem = sem
        self.rr = 0

    def acquire(self, teng):
        i = self.rr % len(self.banks)
        self.rr += 1
        if self.free_at[i] > 0:
            teng.wait_ge(self.sem.h, self.free_at[i])
        return i, self.banks[i]

    def release(self, i):
        self.free_at[i] = self.sem.n


class L0:
    def __init__(self, nc, ctx, io):
        self.nc, self.ctx, self.io = nc, ctx, io
        ent = ctx.enter_context
        for nm in ["x", "w", "pe", "ev", "v", "s", "out", "fa", "f1", "f2",
                   "vt", "et", "pv"]:
            setattr(self, "s_" + nm, Sem(ctx, nc, "sem_" + nm))

        self.acc = ent(nc.sbuf_tensor("acc", [128, DK * TC], F32))
        self.xn = ent(nc.sbuf_tensor("xn", [128, DK * TC], BF16))
        self.ones_c = ent(nc.sbuf_tensor("ones_c", [128, 1], F32))
        self.ones_cb = ent(nc.sbuf_tensor("ones_cb", [128, 1], BF16))
        self.ones_r = ent(nc.sbuf_tensor("ones_r", [1, 128], F32))
        self.ident = ent(nc.sbuf_tensor("sb_ident", [128, 128], BF16))
        self.identf = ent(nc.sbuf_tensor("sb_identf", [128, 128], F32))
        self.dummy = ent(nc.sbuf_tensor("sb_dum", [1, 8], F32))
        self.epsc = ent(nc.sbuf_tensor("epsc", [1, 1], F32))
        self.wr_sb = ent(nc.sbuf_tensor("wr_sb", [128, DK * 8], BF16))
        self.br_sb = ent(nc.sbuf_tensor("br_sb", [8, 1], F32))
        self.sel_sb = ent(nc.sbuf_tensor("sel_sb", [8, NE * 128], BF16))
        self.b2_sb = ent(nc.sbuf_tensor("b2_sb", [8, 1024], BF16))
        self.b1_sb = ent(nc.sbuf_tensor("b1_sb", [128, 256], F32))
        self.negm1 = ent(nc.sbuf_tensor("negm1", [128, NQT], F32))
        self.wsum = ent(nc.sbuf_tensor("wsum", [128, NQT], F32))
        self.wrec = ent(nc.sbuf_tensor("wrec", [128, NQT], F32))
        self.wT_bf = ent(nc.sbuf_tensor("sb_wTbf", [8, TC], BF16))
        self.attn_bb = ent(nc.sbuf_tensor("attn_bb", [1, 3072], BF16))
        self.br_rowb = ent(nc.sbuf_tensor("br_rowb", [1, 8], BF16))
        self.ones_rb = ent(nc.sbuf_tensor("ones_rb", [1, 512], BF16))
        self.proj_b = ent(nc.sbuf_tensor("sb_proj_b", [128, 8], F32))

        self.pool = PsumPool(ctx, nc, self.s_ev, 5)
        self.ptr_bf = [ent(nc.psum_tensor(f"ptrbf{i}", [128, 1024], BF16)) for i in range(2)]
        self.ptr_f = ent(nc.psum_tensor("ptrf", [128, 128], F32))

    def vedge(self, v):
        self.s_v.inc(v.tensor_copy(self.dummy[0:1, 0:1], self.dummy[0:1, 0:1]))
        v.wait_ge(self.s_v.h, self.s_v.n)

    def barrier(self):
        with self.nc.Block() as blk:
            for name in ["tensor", "vector", "scalar", "sync", "gpsimd"]:
                def mk():
                    def f(e):
                        for s in [self.s_x, self.s_w, self.s_pe, self.s_ev,
                                  self.s_v, self.s_s]:
                            s.wait(e)
                    return f
                getattr(blk, name)(mk())

    def xview(self, k, c0, n, halo_buf):
        if c0 < HALO:
            assert c0 + n <= HALO
            return halo_buf[:, k * HALO + c0: k * HALO + c0 + n]
        return self.acc[:, k * TC + (c0 - HALO): k * TC + (c0 - HALO) + n]

    def xnview(self, k, c0, n):
        if c0 < HALO:
            assert c0 + n <= HALO
            return self.xnh[:, k * HALO + c0: k * HALO + c0 + n]
        return self.xn[:, k * TC + (c0 - HALO): k * TC + (c0 - HALO) + n]

    def emit(self, upto=None, nhp=8, attn_stop=None, nsm=999, nwb=2):
        self.nwb = nwb
        self.nhp = nhp
        self.attn_stop = attn_stop
        self.nsm = nsm
        self.upto = upto
        nc = self.nc
        midctx = ExitStack()
        self.xnh = midctx.enter_context(nc.sbuf_tensor("xnh", [128, DK * HALO], BF16))
        with ExitStack() as lnctx:
            ent = lnctx.enter_context
            halo = ent(nc.sbuf_tensor("halo", [128, DK * HALO], F32))
            sqb = ent(nc.sbuf_tensor("sqb", [128, 2 * DK * 512], BF16))
            self.r_m = ent(nc.sbuf_tensor("r_m", [1, 1024], F32))
            self.r_v = ent(nc.sbuf_tensor("r_v", [1, 1024], F32))
            self.r_m2 = ent(nc.sbuf_tensor("r_m2", [1, 1024], F32))
            self.r_rstd = ent(nc.sbuf_tensor("r_rstd", [1, 1024], F32))
            self.r_b = ent(nc.sbuf_tensor("r_b", [1, 1024], F32))
            self.tmp1 = [ent(nc.sbuf_tensor(f"tmp1_{i}", [128, 512], F32))
                         for i in range(2)]
            self._loads(halo)
            if upto == "loads":
                self._store()
            else:
                self._ln(halo, sqb)
                self.barrier()
                if upto == "ln":
                    if self.io.get("dbg") is not None:
                        self._dbg_xn()
                    self._store()
        if upto in ("loads", "ln"):
            midctx.close()
            return
        with ExitStack() as actx:
            ent = actx.enter_context
            ab = dict(
                mask_sb=ent(nc.sbuf_tensor("mask_sb", [128, 3 * NKEY], BF16)),
                q2=ent(nc.sbuf_tensor("q2", [128, TC], BF16)),
                k2=ent(nc.sbuf_tensor("k2", [128, KV], BF16)),
                v2=ent(nc.sbuf_tensor("v2", [128, KV], BF16)),
                vT=ent(nc.sbuf_tensor("vT", [128, 18 * 128], BF16)),
                es=ent(nc.sbuf_tensor("es", [128, 4 * NKEY], BF16)),
                att=ent(nc.sbuf_tensor("att", [128, 4 * NKEY], BF16)),
                attnT=ent(nc.sbuf_tensor("attnT", [128, 2 * NKEY], BF16)),
                negmax=ent(nc.sbuf_tensor("negmax", [128, 8], F32)),
                ssum=ent(nc.sbuf_tensor("ssum", [128, 8], F32)),
                rsum=ent(nc.sbuf_tensor("rsum", [128, 8], F32)),
                outT=ent(nc.sbuf_tensor("outT", [128, DK * TC], BF16)),
                wattn=[ent(nc.sbuf_tensor(f"wattn{i}", [128, 3 * 1024], BF16))
                       for i in range(self.nwb)],
                wprojb=[ent(nc.sbuf_tensor(f"wprojb{i}", [128, 8 * 128], BF16))
                        for i in range(2)],
            )
            self.scT = ent(nc.sbuf_tensor("scT", [8, TC], BF16))
            self.stok = ent(nc.sbuf_tensor("stok", [128, NQT * 8], F32))
            self.top8 = ent(nc.sbuf_tensor("top8", [128, NQT * 8], F32))
            self.keep = ent(nc.sbuf_tensor("keep", [128, NQT * 8], F32))
            self.ew = ent(nc.sbuf_tensor("ew", [128, NQT * 8], F32))
            self.wnorm = ent(nc.sbuf_tensor("wnorm", [128, NQT * 8], F32))
            self._attention(ab)
            if upto == "attn":
                self._store()
            elif upto == "router":
                self._router()
                self._store()
            else:
                self._router()
                self._proj(ab)
                self.barrier()
                if upto == "proj":
                    self._store()
        midctx.close()
        if upto in ("attn", "router", "proj"):
            return
        with ExitStack() as mctx:
            ent = mctx.enter_context
            h_sb = ent(nc.sbuf_tensor("h_sb", [128, 32 * 1024], BF16))
            w1sb = [ent(nc.sbuf_tensor(f"w1sb{i}", [128, 8 * 256], BF16)) for i in range(3)]
            w2sb = [ent(nc.sbuf_tensor(f"w2sb{i}", [128, 32 * 128], BF16)) for i in range(2)]
            wbc = ent(nc.sbuf_tensor("wbc", [128, 2 * 512], BF16))
            self._experts(h_sb, w1sb, w2sb, wbc)
            self._store()

    def _loads(self, halo):
        io = self.io
        with self.nc.Block() as blk:
            @blk.vector
            def _(v):
                v.memset(self.ones_c[:, :], 1.0)
                v.memset(self.ones_cb[:, :], 1.0)
                v.memset(self.dummy[:, :], 0.0)
                v.memset(self.epsc[:, :], EPS)
                v.memset(self.ones_rb[:, :], 1.0)
                self.s_v.inc(v.memset(self.ones_r[:, :], 1.0))

            @blk.sync
            def _(sy):
                for k in range(DK):
                    self.s_x.inc(sy.dma_start(self.acc[:, k * TC:(k + 1) * TC],
                                              io["x_own"][k * 128:(k + 1) * 128, :]), 16)
                    self.s_x.inc(sy.dma_start(halo[:, k * HALO:(k + 1) * HALO],
                                              io["x_halo"][k * 128:(k + 1) * 128, :]), 16)
                for dst, key in [(self.proj_b, "proj_b"),
                                 (self.b1_sb, "b1"),
                                 (self.ident, "ident"), (self.identf, "identf")]:
                    self.s_x.inc(sy.dma_start(dst[:, :], io[key]), 16)
                self.s_x.inc(sy.dma_start(self.sel_sb[:, :], io["sel"]), 16)

            @blk.gpsimd
            def _(g):
                self.s_w.inc(g.dma_start(self.wr_sb[:, :], io["wr"]), 16)
                self.s_w.inc(g.dma_start(self.b2_sb[:, :], io["b2"]), 16)
                self.s_w.inc(g.dma_start(self.attn_bb[:, :], io["attn_brow"]), 16)
                self.s_w.inc(g.dma_start(self.br_rowb[:, :], io["br_row"]), 16)

            @blk.vector
            def _(v):
                self.s_x.wait(v)
                self.s_w.wait(v)
                self.s_v.inc(v.tensor_copy(self.dummy[0:1, 0:1], self.dummy[0:1, 0:1]))

    def _ln(self, halo, sqb):
        nc = self.nc
        info = {}
        with nc.Block() as blk:
            for ti, (c0, n) in enumerate(LNTILES):
                sl = ti % 2

                @blk.scalar
                def _(s, ti=ti, c0=c0, n=n, sl=sl):
                    if ti == 0:
                        self.s_x.wait(s)
                    if ti >= 2:
                        s.wait_ge(self.s_pe.h, info[(ti - 2, "mmq")])
                    for k in range(DK):
                        e = s.activation(sqb[:, (sl * DK + k) * 512:(sl * DK + k) * 512 + n],
                                         self.xview(k, c0, n, halo), AF.Square)
                    info[(ti, "sq")] = self.s_s.inc(e)

                @blk.tensor
                def _(t, ti=ti, c0=c0, n=n, sl=sl):
                    if ti == 0:
                        self.s_x.wait(t)
                        self.s_v.wait(t)
                    ibS, bkS = self.pool.acquire(t)
                    ibQ, bkQ = self.pool.acquire(t)
                    for k in range(DK):
                        t.matmul(bkS[0:1, 0:n], self.ones_c[:, :],
                                 self.xview(k, c0, n, halo),
                                 start=(k == 0), stop=(k == DK - 1))
                    t.wait_ge(self.s_s.h, info[(ti, "sq")])
                    for k in range(DK):
                        mm = t.matmul(bkQ[0:1, 0:n], self.ones_cb[:, :],
                                      sqb[:, (sl * DK + k) * 512:(sl * DK + k) * 512 + n],
                                      start=(k == 0), stop=(k == DK - 1))
                    info[(ti, "mmq")] = self.s_pe.inc(mm)
                    info[(ti, "banks")] = (ibS, bkS, ibQ, bkQ)

                @blk.vector
                def _(v, ti=ti, c0=c0, n=n, sl=sl):
                    ibS, bkS, ibQ, bkQ = info[(ti, "banks")]
                    v.wait_ge(self.s_pe.h, info[(ti, "mmq")])
                    v.tensor_scalar(self.r_m[0:1, sl * 512:sl * 512 + n], bkS[0:1, 0:n],
                                    1.0 / D, None, ALU.mult)
                    self.s_ev.inc(v.tensor_scalar(self.r_v[0:1, sl * 512:sl * 512 + n],
                                                  bkQ[0:1, 0:n], 1.0 / D, None, ALU.mult))
                    self.pool.release(ibS)
                    self.pool.release(ibQ)
                    v.wait_ge(self.s_ev.h, self.s_ev.n)
                    v.tensor_tensor(self.r_m2[0:1, sl * 512:sl * 512 + n],
                                    self.r_m[0:1, sl * 512:sl * 512 + n],
                                    self.r_m[0:1, sl * 512:sl * 512 + n], ALU.mult)
                    self.vedge(v)
                    e = v.tensor_tensor(self.r_v[0:1, sl * 512:sl * 512 + n],
                                        self.r_v[0:1, sl * 512:sl * 512 + n],
                                        self.r_m2[0:1, sl * 512:sl * 512 + n], ALU.subtract)
                    info[(ti, "var")] = self.s_v.inc(e)

                @blk.scalar
                def _(s, ti=ti, c0=c0, n=n, sl=sl):
                    s.wait_ge(self.s_v.h, info[(ti, "var")])
                    self.s_s.inc(s.activation(self.r_m2[0:1, sl * 512:sl * 512 + n],
                                              self.r_v[0:1, sl * 512:sl * 512 + n],
                                              AF.Ln, bias=self.epsc[0:1, 0:1]))
                    s.wait_ge(self.s_s.h, self.s_s.n)
                    e = s.activation(self.r_rstd[0:1, sl * 512:sl * 512 + n],
                                     self.r_m2[0:1, sl * 512:sl * 512 + n], AF.Exp, scale=-0.5)
                    info[(ti, "rstd")] = self.s_s.inc(e)

                @blk.vector
                def _(v, ti=ti, c0=c0, n=n, sl=sl):
                    v.wait_ge(self.s_s.h, info[(ti, "rstd")])
                    e = v.scalar_tensor_tensor(self.r_b[0:1, sl * 512:sl * 512 + n],
                                               self.r_m[0:1, sl * 512:sl * 512 + n], -1.0,
                                               self.r_rstd[0:1, sl * 512:sl * 512 + n],
                                               ALU.mult, ALU.mult)
                    info[(ti, "brow")] = self.s_v.inc(e)

                @blk.tensor
                def _(t, ti=ti, c0=c0, n=n, sl=sl):
                    t.wait_ge(self.s_v.h, info[(ti, "brow")])
                    ibA, bkA = self.pool.acquire(t)
                    ibB, bkB = self.pool.acquire(t)
                    t.matmul(bkA[:, 0:n], self.ones_r[:, :],
                             self.r_rstd[0:1, sl * 512:sl * 512 + n], start=True, stop=True)
                    mm = t.matmul(bkB[:, 0:n], self.ones_r[:, :],
                                  self.r_b[0:1, sl * 512:sl * 512 + n], start=True, stop=True)
                    info[(ti, "bc")] = (self.s_pe.inc(mm), ibA, bkA, ibB, bkB)

                @blk.vector
                def _(v, ti=ti, c0=c0, n=n, sl=sl):
                    cnt, ibA, bkA, ibB, bkB = info[(ti, "bc")]
                    v.wait_ge(self.s_pe.h, cnt)
                    for k0 in range(0, DK, 2):
                        for k in (k0, k0 + 1):
                            v.tensor_tensor(self.tmp1[k % 2][:, 0:n],
                                            self.xview(k, c0, n, halo), bkA[:, 0:n], ALU.mult)
                        self.vedge(v)
                        for k in (k0, k0 + 1):
                            e = v.tensor_tensor(self.xnview(k, c0, n),
                                                self.tmp1[k % 2][:, 0:n], bkB[:, 0:n], ALU.add)
                        self.s_v.inc(e)
                    self.pool.release(ibA)
                    self.pool.release(ibB)
                    self.xn_done = self.s_v.n

    def _attention(self, ab):
        nc, io = self.nc, self.io
        wa_loads = {}
        with nc.Block() as blk:
            @blk.gpsimd
            def _(g):
                for hp in range(min(2, self.nhp)):
                    wa_loads[hp] = self.s_w.inc(g.dma_start(
                        ab["wattn"][hp % self.nwb][:, :], io["attn_w"][hp]), 16)

            @blk.sync
            def _(sy):
                self.s_x.inc(sy.dma_start(ab["mask_sb"][:, :], io["mask"]), 16)

        for hp in range(self.nhp):
            with nc.Block() as blk:
                chunks = ([("q", c0, n) for (c0, n) in QTILES]
                          + [("k", c0, n) for (c0, n) in KVTILES]
                          + [("v", c0, n) for (c0, n) in KVTILES])

                @blk.tensor
                def _(t, hp=hp):
                    t.wait_ge(self.s_v.h, self.xn_done)
                    t.wait_ge(self.s_w.h, wa_loads[hp])
                    t.wait_ge(self.s_et.h, self.s_et.n)

                for ci0 in range(0, len(chunks), 4):
                    grp = []

                    @blk.tensor
                    def _(t, hp=hp, ci0=ci0, grp=grp):
                        nobias = (self.attn_stop == "qkvnb")
                        nopieces = (self.attn_stop in ("qkvnb", "qkvnp"))
                        wsb = ab["wattn"][hp % self.nwb]
                        for (what, c0, n) in chunks[ci0:ci0 + 4]:
                            part = {"q": 0, "k": 1, "v": 2}[what]
                            ib, bk = self.pool.acquire(t)
                            if what == "q" or nopieces:
                                src = (self.xn if what == "q" else None)
                                for k in range(DK):
                                    rhs = (self.xn[:, k * TC + c0: k * TC + c0 + n]
                                           if what == "q" else
                                           self.xn[:, k * TC: k * TC + n])
                                    mm = t.matmul(bk[:, 0:n],
                                                  wsb[:, part * 1024 + k * 128:
                                                      part * 1024 + (k + 1) * 128],
                                                  rhs,
                                                  start=(k == 0),
                                                  stop=(nobias and k == DK - 1))
                            else:
                                pieces = ([(0, 256), (256, 256)] if c0 == 0 else [(c0, n)])
                                for pi, (p0, pn) in enumerate(pieces):
                                    for k in range(DK):
                                        mm = t.matmul(
                                            bk[:, p0 - c0:p0 - c0 + pn],
                                            wsb[:, part * 1024 + k * 128:
                                                part * 1024 + (k + 1) * 128],
                                            self.xnview(k, p0, pn),
                                            start=(pi == 0 and k == 0), stop=False)
                            if not nobias:
                                mm = t.matmul(bk[:, 0:n],
                                              self.attn_bb[0:1, part * 1024 + hp * 128:
                                                           part * 1024 + (hp + 1) * 128],
                                              self.ones_rb[0:1, 0:n],
                                              start=False, stop=True)
                            grp.append((self.s_pe.inc(mm), what, c0, n, ib, bk))

                    @blk.scalar
                    def _(s, hp=hp, ci0=ci0, grp=grp):
                        for (cnt, what, c0, n, ib, bk) in grp:
                            s.wait_ge(self.s_pe.h, cnt)
                            if what == "q":
                                e = s.mul(ab["q2"][:, c0:c0 + n], bk[:, 0:n], ATT_SCALE)
                            elif what == "k":
                                e = s.copy(ab["k2"][:, c0:c0 + n], bk[:, 0:n])
                            else:
                                e = s.copy(ab["v2"][:, c0:c0 + n], bk[:, 0:n])
                            self.s_ev.inc(e)
                            self.pool.release(ib)
                        self.qkv_done = self.s_ev.n

                @blk.vector
                def _(v2, hp=hp):
                    v2.wait_ge(self.s_ev.h, self.qkv_done)
                    self.s_fa.inc(v2.tensor_copy(self.dummy[0:1, 0:1],
                                                 self.dummy[0:1, 0:1]))

                @blk.gpsimd
                def _(g, hp=hp):
                    if hp + 2 < self.nhp:
                        g.wait_ge(self.s_fa.h, self.s_fa.n)
                        wa_loads[hp + 2] = self.s_w.inc(g.dma_start(
                            ab["wattn"][(hp + 2) % self.nwb][:, :],
                            io["attn_w"][hp + 2]), 16)

                if self.attn_stop in ("qkv", "qkvnb", "qkvnp"):
                    continue

                vt_base = self.s_vt.n
                pe_base = self.s_pe.n

                @blk.tensor
                def _(t, hp=hp):
                    t.wait_ge(self.s_ev.h, self.qkv_done)
                    for st in range(18):
                        if st >= 2:
                            t.wait_ge(self.s_vt.h, vt_base + st - 1)
                        self.s_pe.inc(t.transpose(self.ptr_bf[st % 2][:, 0:128],
                                                  ab["v2"][:, st * 128:(st + 1) * 128],
                                                  self.ident[:, :]))

                @blk.vector
                def _(v, hp=hp):
                    for st in range(18):
                        v.wait_ge(self.s_pe.h, pe_base + st + 1)
                        self.s_vt.inc(v.tensor_copy(ab["vT"][:, st * 128:(st + 1) * 128],
                                                    self.ptr_bf[st % 2][:, 0:128]))

                if self.attn_stop == "vtr":
                    continue

                IT = [(qt, hd) for qt in range(NQT) for hd in range(2)]
                IT = IT[:self.nsm * 4]
                et_base = self.s_et.n
                pv_base = self.s_pv.n
                at_all = {}
                tr_all = {}

                @blk.vector
                def _(v, hp=hp):
                    self.s_x.wait(v)

                @blk.tensor
                def _(t, hp=hp):
                    t.wait_ge(self.s_vt.h, self.s_vt.n)

                for i0 in range(0, len(IT), 4):
                    G = list(enumerate(IT))[i0:i0 + 4]
                    sc_info = {}
                    nm_info = {}
                    ex_info = {}
                    at_info = {}
                    tr_info = {}
                    av_info = {}

                    @blk.tensor
                    def _(t, G=G, sc_info=sc_info):
                        for i, (qt, hd) in G:
                            ib, bk = self.pool.acquire(t)
                            mm = t.matmul(bk[:, 0:NKEY],
                                          ab["q2"][hd * 64:(hd + 1) * 64,
                                                   qt * 128:(qt + 1) * 128],
                                          ab["k2"][hd * 64:(hd + 1) * 64,
                                                   qt * 128: qt * 128 + NKEY],
                                          start=True, stop=True)
                            sc_info[i] = (self.s_pe.inc(mm), ib, bk)

                    @blk.vector
                    def _(v, G=G, sc_info=sc_info, nm_info=nm_info):
                        for i, (qt, hd) in G:
                            cnt, ib, bk = sc_info[i]
                            v.wait_ge(self.s_pe.h, cnt)
                            sl = i % 8
                            nm_info[i] = self.s_v.inc(
                                v.reduce_max(ab["negmax"][:, sl:sl + 1], bk[:, 0:NKEY],
                                             AX.X, negate=True))

                    @blk.scalar
                    def _(s, G=G, sc_info=sc_info, nm_info=nm_info, ex_info=ex_info):
                        for i, (qt, hd) in G:
                            cnt, ib, bk = sc_info[i]
                            s.wait_ge(self.s_v.h, nm_info[i])
                            if i >= 4:
                                s.wait_ge(self.s_v.h, at_all[i - 4])
                            sl = i % 8
                            sl4 = i % 4
                            e = s.activation(ab["es"][:, sl4 * NKEY:(sl4 + 1) * NKEY],
                                             bk[:, 0:NKEY], AF.Exp,
                                             bias=ab["negmax"][:, sl:sl + 1])
                            ex_info[i] = self.s_ev.inc(e)
                            self.pool.release(ib)

                    @blk.vector
                    def _(v, G=G, ex_info=ex_info, at_info=at_info):
                        for i, (qt, hd) in G:
                            v.wait_ge(self.s_ev.h, ex_info[i])
                            sl = i % 8
                            sl4 = i % 4
                            esv = ab["es"][:, sl4 * NKEY:(sl4 + 1) * NKEY]
                            atv = ab["att"][:, sl4 * NKEY:(sl4 + 1) * NKEY]
                            mt = min(qt, 2)
                            if i >= 4:
                                v.wait_ge(self.s_pe.h, tr_all[i - 4])
                            v.tensor_tensor(atv, esv,
                                            ab["mask_sb"][:, mt * NKEY:(mt + 1) * NKEY],
                                            ALU.mult)
                            self.s_v.inc(v.reduce_sum(ab["ssum"][:, sl:sl + 1], atv, AX.X))
                        self.vedge(v)
                        for i, (qt, hd) in G:
                            sl = i % 8
                            self.s_v.inc(v.reciprocal(ab["rsum"][:, sl:sl + 1],
                                                      ab["ssum"][:, sl:sl + 1]))
                        self.vedge(v)
                        for i, (qt, hd) in G:
                            sl = i % 8
                            sl4 = i % 4
                            atv = ab["att"][:, sl4 * NKEY:(sl4 + 1) * NKEY]
                            e = v.tensor_tensor(atv, atv,
                                                ab["rsum"][:, sl:sl + 1]
                                                .broadcast_to([128, NKEY]), ALU.mult)
                            at_info[i] = self.s_v.inc(e)
                            at_all[i] = at_info[i]

                    @blk.tensor
                    def _(t, G=G, at_info=at_info, tr_info=tr_info):
                        for i, (qt, hd) in G:
                            t.wait_ge(self.s_v.h, at_info[i])
                            sl4 = i % 4
                            atv = ab["att"][:, sl4 * NKEY:(sl4 + 1) * NKEY]
                            if i >= 2:
                                t.wait_ge(self.s_et.h, et_base + i - 1)
                            for sc in range(3):
                                mm = t.transpose(self.ptr_bf[i % 2][:, sc * 128:(sc + 1) * 128],
                                                 atv[:, sc * 128:(sc + 1) * 128],
                                                 self.ident[:, :])
                            tr_info[i] = self.s_pe.inc(mm)
                            tr_all[i] = tr_info[i]

                    @blk.vector
                    def _(v, G=G, tr_info=tr_info):
                        for i, (qt, hd) in G:
                            v.wait_ge(self.s_pe.h, tr_info[i])
                            if i >= 2:
                                v.wait_ge(self.s_pv.h, pv_base + i - 1)
                            sl2 = i % 2
                            self.s_et.inc(v.tensor_copy(
                                ab["attnT"][:, sl2 * NKEY:(sl2 + 1) * NKEY],
                                self.ptr_bf[i % 2][:, 0:NKEY]))

                    @blk.tensor
                    def _(t, G=G, av_info=av_info):
                        for i, (qt, hd) in G:
                            t.wait_ge(self.s_et.h, et_base + i + 1)
                            sl2 = i % 2
                            ib, bk = self.pool.acquire(t)
                            for sc in range(3):
                                mm = t.matmul(bk[0:64, 0:128],
                                              ab["vT"][:, (qt + sc) * 128 + hd * 64:
                                                       (qt + sc) * 128 + hd * 64 + 64],
                                              ab["attnT"][:, sl2 * NKEY + sc * 128:
                                                          sl2 * NKEY + (sc + 1) * 128],
                                              start=(sc == 0), stop=(sc == 2))
                            av_info[i] = (self.s_pv.inc(mm), ib, bk)

                    @blk.vector
                    def _(v, G=G, av_info=av_info):
                        for i, (qt, hd) in G:
                            cnt, ib, bk = av_info[i]
                            v.wait_ge(self.s_pv.h, cnt)
                            e = v.tensor_copy(
                                ab["outT"][hd * 64:(hd + 1) * 64,
                                          hp * TC + qt * 128: hp * TC + (qt + 1) * 128],
                                bk[0:64, 0:128])
                            self.s_ev.inc(e)
                            self.pool.release(ib)
                        self.attn_done = self.s_ev.n

    def _router(self):
        nc = self.nc
        with nc.Block() as blk:
            sc_grp = []

            @blk.tensor
            def _(t):
                t.wait_ge(self.s_v.h, self.xn_done)
                for (c0, n) in QTILES:
                    ib, bk = self.pool.acquire(t)
                    for k in range(DK):
                        t.matmul(bk[0:8, 0:n], self.wr_sb[:, k * 8:(k + 1) * 8],
                                 self.xn[:, k * TC + c0: k * TC + c0 + n],
                                 start=(k == 0), stop=False)
                    mm = t.matmul(bk[0:8, 0:n], self.br_rowb[0:1, 0:8],
                                  self.ones_rb[0:1, 0:n], start=False, stop=True)
                    sc_grp.append((self.s_pe.inc(mm), c0, n, ib, bk))

            @blk.scalar
            def _(s):
                for (cnt, c0, n, ib, bk) in sc_grp:
                    s.wait_ge(self.s_pe.h, cnt)
                    e = s.copy(self.scT[0:8, c0:c0 + n], bk[0:8, 0:n])
                    self.s_ev.inc(e)
                    self.pool.release(ib)
                self.scT_done = self.s_ev.n

            tr_cnt = {}
            et_base = self.s_et.n

            @blk.tensor
            def _(t):
                t.wait_ge(self.s_ev.h, self.scT_done)
                for tt in range(NQT):
                    if tt >= 1:
                        t.wait_ge(self.s_et.h, et_base + tt)
                    mm = t.transpose(self.ptr_bf[0][:, 0:8],
                                     self.scT[0:8, tt * 128:(tt + 1) * 128],
                                     self.ident[0:8, 0:8])
                    tr_cnt[tt] = self.s_pe.inc(mm)

            @blk.vector
            def _(v):
                for tt in range(NQT):
                    v.wait_ge(self.s_pe.h, tr_cnt[tt])
                    self.s_et.inc(v.tensor_copy(self.stok[:, tt * 8:(tt + 1) * 8],
                                                self.ptr_bf[0][:, 0:8]))
                self.stok_done = self.s_et.n

            @blk.vector
            def _(v):
                v.wait_ge(self.s_et.h, self.stok_done)
                for tt in range(NQT):
                    e = v.max(self.top8[:, tt * 8:(tt + 1) * 8],
                              self.stok[:, tt * 8:(tt + 1) * 8])
                self.s_v.inc(e)
                v.wait_ge(self.s_v.h, self.s_v.n)
                for tt in range(NQT):
                    v.tensor_tensor(self.keep[:, tt * 8:(tt + 1) * 8],
                                    self.stok[:, tt * 8:(tt + 1) * 8],
                                    self.top8[:, tt * 8 + 1:tt * 8 + 2].broadcast_to([128, 8]),
                                    ALU.is_ge)
                    e = v.tensor_scalar(self.negm1[:, tt:tt + 1],
                                        self.top8[:, tt * 8:tt * 8 + 1], -1.0, None, ALU.mult)
                self.nm1_done = self.s_v.inc(e)

            @blk.scalar
            def _(s):
                s.wait_ge(self.s_v.h, self.nm1_done)
                for tt in range(NQT):
                    e = s.activation(self.ew[:, tt * 8:(tt + 1) * 8],
                                     self.stok[:, tt * 8:(tt + 1) * 8], AF.Exp,
                                     bias=self.negm1[:, tt:tt + 1])
                self.ew_done = self.s_s.inc(e)

            @blk.vector
            def _(v):
                v.wait_ge(self.s_s.h, self.ew_done)
                for tt in range(NQT):
                    e = v.tensor_tensor(self.ew[:, tt * 8:(tt + 1) * 8],
                                        self.ew[:, tt * 8:(tt + 1) * 8],
                                        self.keep[:, tt * 8:(tt + 1) * 8], ALU.mult)
                self.s_v.inc(e)
                v.wait_ge(self.s_v.h, self.s_v.n)
                for tt in range(NQT):
                    e = v.reduce_sum(self.wsum[:, tt:tt + 1],
                                     self.ew[:, tt * 8:(tt + 1) * 8], AX.X)
                self.s_v.inc(e)
                v.wait_ge(self.s_v.h, self.s_v.n)
                for tt in range(NQT):
                    e = v.reciprocal(self.wrec[:, tt:tt + 1], self.wsum[:, tt:tt + 1])
                self.s_v.inc(e)
                v.wait_ge(self.s_v.h, self.s_v.n)
                for tt in range(NQT):
                    e = v.tensor_tensor(self.wnorm[:, tt * 8:(tt + 1) * 8],
                                        self.ew[:, tt * 8:(tt + 1) * 8],
                                        self.wrec[:, tt:tt + 1].broadcast_to([128, 8]),
                                        ALU.mult)
                self.wnorm_done = self.s_v.inc(e)

            wtr = {}
            wet_base = self.s_et.n

            @blk.tensor
            def _(t):
                t.wait_ge(self.s_v.h, self.wnorm_done)
                for tt in range(NQT):
                    if tt >= 1:
                        t.wait_ge(self.s_et.h, wet_base + tt)
                    mm = t.transpose(self.ptr_f[0:8, 0:128],
                                     self.wnorm[:, tt * 8:(tt + 1) * 8], self.identf[:, :])
                    wtr[tt] = self.s_pe.inc(mm)

            @blk.scalar
            def _(s):
                for tt in range(NQT):
                    s.wait_ge(self.s_pe.h, wtr[tt])
                    e = s.copy(self.wT_bf[0:8, tt * 128:(tt + 1) * 128],
                               self.ptr_f[0:8, 0:128])
                    self.s_et.inc(e)
                self.wT_done = self.s_et.n

    def _proj(self, ab):
        nc = self.nc
        io = self.io
        with nc.Block() as blk:
            wp_loads = {}

            @blk.gpsimd
            def _(g):
                for dout in range(2):
                    wp_loads[dout] = self.s_w.inc(g.dma_start(
                        ab["wprojb"][dout % 2][:, :], io["proj_w"][dout]), 16)

            @blk.tensor
            def _(t):
                t.wait_ge(self.s_ev.h, self.attn_done)

            for dout in range(DK):
                grp = []

                @blk.tensor
                def _(t, dout=dout, grp=grp):
                    t.wait_ge(self.s_w.h, wp_loads[dout])
                    wsb = ab["wprojb"][dout % 2]
                    for (c0, n) in QTILES:
                        ib, bk = self.pool.acquire(t)
                        for k in range(DK):
                            mm = t.matmul(bk[:, 0:n],
                                          wsb[:, k * 128:(k + 1) * 128],
                                          ab["outT"][:, k * TC + c0: k * TC + c0 + n],
                                          start=(k == 0), stop=(k == DK - 1))
                        grp.append((self.s_pe.inc(mm), c0, n, ib, bk))

                @blk.vector
                def _(v, dout=dout, grp=grp):
                    for (cnt, c0, n, ib, bk) in grp:
                        v.wait_ge(self.s_pe.h, cnt)
                        e = v.scalar_tensor_tensor(
                            self.acc[:, dout * TC + c0: dout * TC + c0 + n],
                            bk[:, 0:n], self.proj_b[:, dout:dout + 1],
                            self.acc[:, dout * TC + c0: dout * TC + c0 + n],
                            ALU.add, ALU.add)
                        self.s_ev.inc(e)
                        self.pool.release(ib)
                    self.s_fa.inc(v.tensor_copy(self.dummy[0:1, 1:2], self.dummy[0:1, 1:2]))
                    self.proj_done = self.s_ev.n

                @blk.gpsimd
                def _(g, dout=dout):
                    if dout + 2 < DK:
                        g.wait_ge(self.s_fa.h, self.s_fa.n)
                        wp_loads[dout + 2] = self.s_w.inc(g.dma_start(
                            ab["wprojb"][(dout + 2) % 2][:, :],
                            io["proj_w"][dout + 2]), 16)

    def _experts(self, h_sb, w1sb, w2sb, wbc):
        nc, io = self.nc, self.io
        order = [(th, e) for th in range(2) for e in range(NE)]
        w1_loads = {}
        w2_loads = {}

        def w1_key(gi):
            oi2, band = gi // 16, gi % 16
            th2, e2 = order[oi2]
            return (th2, e2, band), e2, band

        def w2_key(gi):
            oi2, db = gi // 8, gi % 8
            th2, e2 = order[oi2]
            return (th2, e2, db), e2, db

        with nc.Block() as blk:
            @blk.gpsimd
            def _(g):
                for gi in range(3):
                    key, e2, band = w1_key(gi)
                    w1_loads[key] = self.s_w.inc(g.dma_start(
                        w1sb[gi % 3][:, :], io["w1"][e2, band]), 16)
                for gi in range(2):
                    key, e2, db = w2_key(gi)
                    w2_loads[key] = self.s_w.inc(g.dma_start(
                        w2sb[gi % 2][:, :], io["w2"][e2, db]), 16)

        NW1 = len(order) * 16
        NW2 = len(order) * 8
        for oi, (th, e) in enumerate(order):
            with nc.Block() as blk:
                @blk.gpsimd
                def _(g, oi=oi):
                    for gi in range(oi * 16 + 3, min((oi + 1) * 16 + 3, NW1)):
                        g.wait_ge(self.s_f1.h, gi - 2)
                        key, e2, band = w1_key(gi)
                        w1_loads[key] = self.s_w.inc(g.dma_start(
                            w1sb[gi % 3][:, :], io["w1"][e2, band]), 16)
                    for gi in range(oi * 8 + 2, min((oi + 1) * 8 + 2, NW2)):
                        g.wait_ge(self.s_f2.h, gi - 1)
                        key, e2, db = w2_key(gi)
                        w2_loads[key] = self.s_w.inc(g.dma_start(
                            w2sb[gi % 2][:, :], io["w2"][e2, db]), 16)

                wb_cnt = {}

                @blk.tensor
                def _(t, th=th, e=e, oi=oi):
                    t.wait_ge(self.s_et.h, self.wT_done)
                    if oi == 0:
                        t.wait_ge(self.s_ev.h, self.proj_done)
                    for tci in range(2):
                        c0 = th * 1024 + tci * 512
                        ib, bk = self.pool.acquire(t)
                        mm = t.matmul(bk[:, 0:512], self.sel_sb[0:8, e * 128:(e + 1) * 128],
                                      self.wT_bf[0:8, c0:c0 + 512], start=True, stop=True)
                        wb_cnt[tci] = (self.s_pe.inc(mm), ib, bk)

                @blk.scalar
                def _(s, th=th, e=e):
                    for tci in range(2):
                        cnt, ib, bk = wb_cnt[tci]
                        s.wait_ge(self.s_pe.h, cnt)
                        e2 = s.copy(wbc[:, tci * 512:(tci + 1) * 512], bk[:, 0:512])
                        self.s_ev.inc(e2)
                        self.pool.release(ib)
                    self.wbc_done = self.s_ev.n

                @blk.tensor
                def _(t, oi=oi):
                    t.wait_ge(self.s_ev.h, self.wbc_done)
                    if oi >= 1:
                        t.wait_ge(self.s_pe.h, self.w2_mm_done)

                for m in range(32):
                    h_cnt = {}

                    @blk.tensor
                    def _(t, th=th, e=e, oi=oi, m=m, h_cnt=h_cnt):
                        band = m // 2
                        idx = oi * 16 + band
                        t.wait_ge(self.s_w.h, w1_loads[(th, e, band)])
                        wsb = w1sb[idx % 3]
                        for tci in range(2):
                            c0 = th * 1024 + tci * 512
                            ib, bk = self.pool.acquire(t)
                            for k in range(DK):
                                mm = t.matmul(
                                    bk[:, 0:512],
                                    wsb[:, k * 256 + (m % 2) * 128:
                                        k * 256 + (m % 2) * 128 + 128],
                                    self.xn[:, k * TC + c0: k * TC + c0 + 512],
                                    start=(k == 0), stop=(k == DK - 1))
                            h_cnt[tci] = (self.s_pe.inc(mm), ib, bk)

                    @blk.scalar
                    def _(s, th=th, e=e, m=m, h_cnt=h_cnt):
                        for tci in range(2):
                            cnt, ib, bk = h_cnt[tci]
                            s.wait_ge(self.s_pe.h, cnt)
                            e2 = s.activation(h_sb[:, m * 1024 + tci * 512:
                                                   m * 1024 + (tci + 1) * 512],
                                              bk[:, 0:512], AF.Relu,
                                              bias=self.b1_sb[:, e * 32 + m: e * 32 + m + 1])
                            h_cnt[(tci, "ev")] = self.s_ev.inc(e2)
                            self.pool.release(ib)
                        if m % 2 == 1:
                            self.s_f1.inc(s.copy(self.dummy[0:1, 2:3],
                                                 self.dummy[0:1, 2:3]))

                    @blk.vector
                    def _(v, th=th, e=e, m=m, h_cnt=h_cnt):
                        for tci in range(2):
                            v.wait_ge(self.s_ev.h, h_cnt[(tci, "ev")])
                            sl = h_sb[:, m * 1024 + tci * 512: m * 1024 + (tci + 1) * 512]
                            e2 = v.tensor_tensor(sl, sl, wbc[:, tci * 512:(tci + 1) * 512],
                                                 ALU.mult)
                            self.hw_done = self.s_v.inc(e2)

                for db in range(8):
                    y_cnt = {}

                    @blk.tensor
                    def _(t, th=th, e=e, oi=oi, db=db, y_cnt=y_cnt):
                        if db == 0:
                            t.wait_ge(self.s_v.h, self.hw_done)
                        idx = oi * 8 + db
                        t.wait_ge(self.s_w.h, w2_loads[(th, e, db)])
                        wsb = w2sb[idx % 2]
                        for tci in range(2):
                            c0 = th * 1024 + tci * 512
                            ib, bk = self.pool.acquire(t)
                            if e == 0:
                                t.matmul(bk[:, 0:512],
                                         self.b2_sb[0:8, db * 128:(db + 1) * 128],
                                         self.wT_bf[0:8, c0:c0 + 512],
                                         start=True, stop=False)
                            for m in range(32):
                                mm = t.matmul(bk[:, 0:512],
                                              wsb[:, m * 128:(m + 1) * 128],
                                              h_sb[:, m * 1024 + tci * 512:
                                                   m * 1024 + (tci + 1) * 512],
                                              start=(m == 0 and e != 0), stop=(m == 31))
                            y_cnt[tci] = (self.s_pe.inc(mm), ib, bk)
                        self.w2_mm_done = self.s_pe.n

                    @blk.vector
                    def _(v, th=th, e=e, db=db, y_cnt=y_cnt):
                        for tci in range(2):
                            cnt, ib, bk = y_cnt[tci]
                            v.wait_ge(self.s_pe.h, cnt)
                            c0 = th * 1024 + tci * 512
                            sl = self.acc[:, db * TC + c0: db * TC + c0 + 512]
                            e2 = v.tensor_tensor(sl, sl, bk[:, 0:512], ALU.add)
                            self.s_ev.inc(e2)
                            self.pool.release(ib)
                        self.s_f2.inc(v.tensor_copy(self.dummy[0:1, 3:4],
                                                    self.dummy[0:1, 3:4]))
                        self.moe_acc_done = self.s_ev.n

    def _dbg_xn(self):
        with self.nc.Block() as blk:
            @blk.sync
            def _(sy):
                sy.wait_ge(self.s_v.h, self.xn_done)
                for k in range(DK):
                    self.s_out.inc(sy.dma_start(
                        self.io["dbg"][k * 128:(k + 1) * 128, :],
                        self.xn[:, k * TC:(k + 1) * TC]), 16)

    def _store(self):
        with self.nc.Block() as blk:
            @blk.sync
            def _(sy):
                sy.wait_ge(self.s_ev.h, getattr(self, "moe_acc_done", self.s_ev.n))
                sy.wait_ge(self.s_v.h, self.s_v.n)
                sy.wait_ge(self.s_x.h, self.s_x.n)
                for k in range(DK):
                    self.s_out.inc(sy.dma_start(
                        self.io["x1"][k * 128:(k + 1) * 128, :],
                        self.acc[:, k * TC:(k + 1) * TC]), 16)
                sy.wait_ge(self.s_out.h, self.s_out.n)



def _prep_l0_core(inputs, c):
    b, g = c & 3, c >> 2
    x = np.asarray(inputs["L0_attn_x"][b], np.float32)
    xT = np.ascontiguousarray(x.T)
    own = np.ascontiguousarray(xT[:, g * TC:(g + 1) * TC])
    if g == 0:
        haloa = np.zeros((D, HALO), np.float32)
    else:
        haloa = np.ascontiguousarray(xT[:, g * TC - HALO: g * TC])
    i = np.arange(128)[:, None]
    j = np.arange(NKEY)[None, :]
    base = (j - i >= 0) & (j - i <= 256)
    mask = np.zeros((3, 128, NKEY), np.float32)
    for qt in range(3):
        m = base.copy()
        if g == 0 and qt < 2:
            m = m & (qt * 128 + j >= HALO)
        mask[qt] = m
    attn_brow = np.ascontiguousarray(
        np.asarray(inputs["L0_attn_b_attn"], np.float32).reshape(1, 3072))
    proj_b = np.ascontiguousarray(
        np.asarray(inputs["L0_attn_b_proj"], np.float32).reshape(8, 128).T)
    b1 = np.ascontiguousarray(
        np.asarray(inputs["L0_ffn_b1"], np.float32).reshape(8, 32, 128)
        .transpose(2, 0, 1).reshape(128, 256))
    sel = np.zeros((8, 8 * 128), np.float32)
    for e in range(8):
        sel[e, e * 128:(e + 1) * 128] = 1.0
    mask2 = np.ascontiguousarray(mask.transpose(1, 0, 2).reshape(128, 3 * NKEY))
    W_attn = np.asarray(inputs["L0_attn_W_attn"], np.float32)
    wa = W_attn.reshape(8, 128, 3, 8, 128).transpose(3, 1, 0, 2, 4)
    wa = W_attn.reshape(8, 128, 3, 8, 128).transpose(3, 1, 2, 0, 4)
    wa = np.ascontiguousarray(wa.reshape(8, 128, 3 * 1024))
    W_proj = np.asarray(inputs["L0_attn_W_proj"], np.float32)
    wp = W_proj.reshape(8, 128, 8, 128).transpose(2, 1, 0, 3)
    wp = np.ascontiguousarray(wp.reshape(8, 128, 1024))
    W1 = np.asarray(inputs["L0_ffn_W1"], np.float32)
    w1b = W1.reshape(8, 8, 128, 16, 256).transpose(0, 3, 2, 1, 4)
    w1b = np.ascontiguousarray(w1b.reshape(8, 16, 128, 8 * 256))
    W2 = np.asarray(inputs["L0_ffn_W2"], np.float32)
    w2b = W2.reshape(8, 32, 128, 8, 128).transpose(0, 3, 2, 1, 4)
    w2b = np.ascontiguousarray(w2b.reshape(8, 8, 128, 32 * 128))
    Wr = np.asarray(inputs["L0_ffn_Wr"], np.float32)
    wrb = np.ascontiguousarray(Wr.reshape(8, 128, 8).transpose(1, 0, 2).reshape(128, 64))
    return dict(
        x_own=own, x_halo=haloa, mask=mask2.astype(BF),
        attn_w=wa,
        attn_brow=attn_brow,
        br_row=np.ascontiguousarray(np.asarray(inputs["L0_ffn_br"], np.float32).reshape(1, 8)),
        proj_w=wp,
        proj_b=proj_b,
        wr=wrb,
        br=np.ascontiguousarray(np.asarray(inputs["L0_ffn_br"], np.float32).reshape(8, 1)),
        w1=w1b,
        b1=b1,
        w2=w2b,
        b2=np.ascontiguousarray(np.asarray(inputs["L0_ffn_b2"], np.float32)),
        sel=sel.astype(BF),
        ident=np.eye(128, dtype=np.float32).astype(BF),
        identf=np.eye(128, dtype=np.float32),
    )


def _build_l0(upto=None, nhp=8, attn_stop=None, nsm=999, nwb=2):
    nc = bass.Bass("TRN2", target_bir_lowering=False)
    io = {}
    io["x_own"] = nc.dram_tensor("x_own", [D, TC], F32, kind="ExternalInput").ap()
    io["x_halo"] = nc.dram_tensor("x_halo", [D, HALO], F32, kind="ExternalInput").ap()
    io["mask"] = nc.dram_tensor("mask", [128, 3 * NKEY], BF16, kind="ExternalInput").ap()
    io["attn_w"] = nc.dram_tensor("attn_w", [8, 128, 3 * 1024], F32, kind="ExternalInput").ap()
    io["attn_brow"] = nc.dram_tensor("attn_brow", [1, 3072], F32, kind="ExternalInput").ap()
    io["br_row"] = nc.dram_tensor("br_row", [1, 8], F32, kind="ExternalInput").ap()
    io["proj_w"] = nc.dram_tensor("proj_w", [8, 128, 1024], F32, kind="ExternalInput").ap()
    io["proj_b"] = nc.dram_tensor("proj_b", [128, 8], F32, kind="ExternalInput").ap()
    io["wr"] = nc.dram_tensor("wr", [128, 64], F32, kind="ExternalInput").ap()
    io["br"] = nc.dram_tensor("br", [8, 1], F32, kind="ExternalInput").ap()
    io["w1"] = nc.dram_tensor("w1", [NE, 16, 128, 8 * 256], F32, kind="ExternalInput").ap()
    io["b1"] = nc.dram_tensor("b1", [128, 256], F32, kind="ExternalInput").ap()
    io["w2"] = nc.dram_tensor("w2", [NE, 8, 128, 32 * 128], F32, kind="ExternalInput").ap()
    io["b2"] = nc.dram_tensor("b2", [8, 1024], F32, kind="ExternalInput").ap()
    io["sel"] = nc.dram_tensor("sel", [8, 8 * 128], BF16, kind="ExternalInput").ap()
    io["ident"] = nc.dram_tensor("ident", [128, 128], BF16, kind="ExternalInput").ap()
    io["identf"] = nc.dram_tensor("identf", [128, 128], F32, kind="ExternalInput").ap()
    io["x1"] = nc.dram_tensor("x1", [D, TC], F32, kind="ExternalOutput").ap()
    if upto in ("ln",):
        io["dbg"] = nc.dram_tensor("dbg", [D, TC], BF16, kind="ExternalOutput").ap()
    with ExitStack() as ctx:
        l0 = L0(nc, ctx, io)
        l0.emit(upto=upto, nhp=nhp, attn_stop=attn_stop, nsm=nsm, nwb=nwb)
    return nc


def dev_run_l0(inputs, upto=None, nhp=8, attn_stop=None, nsm=999, nwb=2):
    nc = _build_l0(upto=upto, nhp=nhp, attn_stop=attn_stop, nsm=nsm, nwb=nwb)
    in_maps = [_prep_l0_core(inputs, c) for c in range(8)]
    res = run_bass_kernel_spmd(nc, in_maps, core_ids=list(range(8)))
    return res.results





NCH = 8
CH = 512
ET = 16
NS = 16
NG = 4
R = 64


class L1:

    def __init__(self, nc, ctx, io):
        self.nc, self.ctx, self.io = nc, ctx, io
        ent = ctx.enter_context
        for nm in ["x", "w", "pe", "ev", "v", "s", "out", "fa", "f1", "f2",
                   "vt", "et", "pv"]:
            setattr(self, "s_" + nm, Sem(ctx, nc, "sem_" + nm))
        self.ones_c = ent(nc.sbuf_tensor("ones_c", [128, 1], F32))
        self.ones_cb = ent(nc.sbuf_tensor("ones_cb", [128, 1], BF16))
        self.ones_r = ent(nc.sbuf_tensor("ones_r", [1, 128], F32))
        self.ones_rb = ent(nc.sbuf_tensor("ones_rb", [1, 512], BF16))
        self.ident = ent(nc.sbuf_tensor("sb_ident", [128, 128], BF16))
        self.identf = ent(nc.sbuf_tensor("sb_identf", [128, 128], F32))
        self.dummy = ent(nc.sbuf_tensor("sb_dum", [1, 8], F32))
        self.epsc = ent(nc.sbuf_tensor("epsc", [1, 1], F32))
        self.normw = ent(nc.sbuf_tensor("sb_normw", [128, 8], F32))
        self.normf = ent(nc.sbuf_tensor("sb_normf", [128, 8], F32))
        self.gamma = ent(nc.sbuf_tensor("sb_gamma", [128, 8], F32))
        self.beta = ent(nc.sbuf_tensor("sb_beta", [128, 8], F32))
        self.sel8 = ent(nc.sbuf_tensor("sb_sel8", [8, 8 * 128], BF16))
        self.rwT = ent(nc.sbuf_tensor("sb_rwT", [128, 64], BF16))
        self.wT_bf = ent(nc.sbuf_tensor("sb_wTbf", [8, TC], BF16))
        self.r_m = ent(nc.sbuf_tensor("r_m", [1, 1024], F32))
        self.r_v = ent(nc.sbuf_tensor("r_v", [1, 1024], F32))
        self.r_m2 = ent(nc.sbuf_tensor("r_m2", [1, 1024], F32))
        self.r_rstd = ent(nc.sbuf_tensor("r_rstd", [1, 1024], F32))
        self.r_b = ent(nc.sbuf_tensor("r_b", [1, 1024], F32))
        self.tmp1 = [ent(nc.sbuf_tensor(f"tmp1_{i}", [128, 512], F32)) for i in range(2)]
        self.pool = PsumPool(ctx, nc, self.s_ev, 6)
        self.ptr_f = ent(nc.psum_tensor("ptrf", [128, 128], F32))
        self.ptr_bf = ent(nc.psum_tensor("ptrbf0", [128, 1024], BF16))
        self.xn1_dram = nc.dram_tensor("xn1_dram", [D, LSEQ], BF16).ap()
        self.oacc = nc.dram_tensor("oacc", [D, TC], F32, kind="ExternalOutput").ap()

    def vedge(self, v):
        self.s_v.inc(v.tensor_copy(self.dummy[0:1, 0:1], self.dummy[0:1, 0:1]))
        v.wait_ge(self.s_v.h, self.s_v.n)

    def sedge(self, s):
        self.s_s.inc(s.copy(self.dummy[0:1, 1:2], self.dummy[0:1, 1:2]))
        s.wait_ge(self.s_s.h, self.s_s.n)

    def barrier(self):
        with self.nc.Block() as blk:
            for name in ["tensor", "vector", "scalar", "sync", "gpsimd"]:
                def mk():
                    def f(e):
                        for s in [self.s_x, self.s_w, self.s_pe, self.s_ev,
                                  self.s_v, self.s_s, self.s_out]:
                            s.wait(e)
                    return f
                getattr(blk, name)(mk())

    def emit(self, upto=None):
        nc = self.nc
        with ExitStack() as c1:
            ent = c1.enter_context
            x1f = ent(nc.sbuf_tensor("x1f", [128, DK * 512], F32))
            sqb = ent(nc.sbuf_tensor("sqb", [128, DK * 512], BF16))
            xnc = ent(nc.sbuf_tensor("xnc", [128, DK * 512], BF16))
            self._loads()
            self._ln(x1f, sqb, xnc)
        self.barrier()
        if upto == "ln":
            self._store(skip_contrib=True, have_acc=False)
            return
        with ExitStack() as c2:
            ent = c2.enter_context
            mb = dict(
                xn1ch=ent(nc.sbuf_tensor("xn1ch", [128, DK * CH], BF16)),
                xmv=ent(nc.sbuf_tensor("xmv", [128, DK * CH], BF16)),
                xcr=ent(nc.sbuf_tensor("xcr", [128, ET * (CH + 3)], BF16)),
                xc=ent(nc.sbuf_tensor("xc", [128, ET * CH], BF16)),
                dtu=ent(nc.sbuf_tensor("dtu", [128, 2 * CH], BF16)),
                Bb=ent(nc.sbuf_tensor("Bb", [128, NS * CH], BF16)),
                Cb=ent(nc.sbuf_tensor("Cb", [128, NS * CH], BF16)),
                dA=ent(nc.sbuf_tensor("dA", [128, NG * CH], F32)),
                uB=ent(nc.sbuf_tensor("uB", [128, NG * CH], BF16)),
                hh=ent(nc.sbuf_tensor("hh", [128, NG * CH], BF16)),
                                yy=ent(nc.sbuf_tensor("yy", [128, CH], F32)),
                carry=ent(nc.sbuf_tensor("carry", [128, 256], F32)),
                vout=ent(nc.sbuf_tensor("vout", [128, ET * CH], BF16)),
                ot=ent(nc.sbuf_tensor("ot", [128, CH], F32)),
                otr=ent(nc.sbuf_tensor("otr", [128, 2 * CH], F32)),
                wip=[ent(nc.sbuf_tensor(f"wip{i}", [128, 8 * 128], BF16)) for i in range(2)],
                wop=[ent(nc.sbuf_tensor(f"wop{i}", [128, 16 * 128], BF16)) for i in range(2)],
                xdbl=ent(nc.sbuf_tensor("xdbl", [64, CH], BF16)),
                Brow=ent(nc.sbuf_tensor("Brow", [16, CH], BF16)),
                Crow=ent(nc.sbuf_tensor("Crow", [16, CH], BF16)),
                alog=ent(nc.sbuf_tensor("sb_alog", [128, 256], F32)),
                A=ent(nc.sbuf_tensor("sb_A", [128, 256], F32)),
                dp=ent(nc.sbuf_tensor("sb_dp", [128, 16], F32)),
                convw=ent(nc.sbuf_tensor("sb_convw", [128, 64], F32)),
                convb=ent(nc.sbuf_tensor("sb_convb", [128, 16], F32)),
                dtb_row=ent(nc.sbuf_tensor("sb_dtbrow", [1, 2048], BF16)),
                sel16=ent(nc.sbuf_tensor("sb_sel16", [16, 16 * 128], BF16)),
                xprojT=ent(nc.sbuf_tensor("sb_xprojT", [128, 16 * 96], BF16)),
                dtwT=ent(nc.sbuf_tensor("sb_dtwT", [64, 2048], BF16)),
                y2=ent(nc.sbuf_tensor("y2", [128, CH], F32)),
                t2a=ent(nc.sbuf_tensor("t2a", [128, CH], F32)),
                t2b=ent(nc.sbuf_tensor("t2b", [128, CH], F32)),
            )
            self._mamba(mb, upto)
        self.barrier()
        if upto in ("inproj", "conv", "xproj", "scan", "mamba"):
            self._store(skip_contrib=False, have_acc=False)
            return
        with ExitStack() as c3:
            ent = c3.enter_context
            self.acc = ent(nc.sbuf_tensor("acc", [128, DK * TC], BF16))
            h_sb = ent(nc.sbuf_tensor("h_sb", [128, 32 * 1024], BF16))
            w1sb = [ent(nc.sbuf_tensor(f"w1sb{i}", [128, 8 * 256], BF16)) for i in range(3)]
            w2sb = [ent(nc.sbuf_tensor(f"w2sb{i}", [128, 32 * 128], BF16)) for i in range(2)]
            wbc = ent(nc.sbuf_tensor("wbc", [128, 2 * 512], BF16))
            xnf = ent(nc.sbuf_tensor("xnf", [128, DK * 1024], BF16))
            sqf = ent(nc.sbuf_tensor("sqf", [128, DK * 512], BF16))
            xnin = ent(nc.sbuf_tensor("xnin", [128, DK * 512], BF16))
            stok = ent(nc.sbuf_tensor("stok", [128, 16 * 8], F32))
            top8 = ent(nc.sbuf_tensor("top8", [128, 16 * 8], F32))
            keep = ent(nc.sbuf_tensor("keep", [128, 16 * 8], F32))
            ew = ent(nc.sbuf_tensor("ew", [128, 16 * 8], F32))
            wnorm = ent(nc.sbuf_tensor("wnorm", [128, 16 * 8], F32))
            negm1 = ent(nc.sbuf_tensor("negm1", [128, 16], F32))
            wsum = ent(nc.sbuf_tensor("wsum", [128, 16], F32))
            wrec = ent(nc.sbuf_tensor("wrec", [128, 16], F32))
            scT = ent(nc.sbuf_tensor("scT", [8, TC], BF16))
            fb = dict(h_sb=h_sb, w1sb=w1sb, w2sb=w2sb, wbc=wbc, xnf=xnf, sqf=sqf,
                      xnin=xnin, stok=stok, top8=top8, keep=keep, ew=ew, wnorm=wnorm,
                      negm1=negm1, wsum=wsum, wrec=wrec, scT=scT)
            self._ffn(fb)
        self._store(skip_contrib=False)

    def _loads(self):
        io = self.io
        with self.nc.Block() as blk:
            @blk.vector
            def _(v):
                v.memset(self.ones_c[:, :], 1.0)
                v.memset(self.ones_cb[:, :], 1.0)
                v.memset(self.ones_rb[:, :], 1.0)
                v.memset(self.dummy[:, :], 0.0)
                v.memset(self.epsc[:, :], EPS)
                self.s_v.inc(v.memset(self.ones_r[:, :], 1.0))

            @blk.sync
            def _(sy):
                for dst, key in [(self.ident, "ident"), (self.identf, "identf"),
                                 (self.normw, "m_norm"), (self.normf, "m_normf"),
                                 (self.gamma, "gamma"), (self.beta, "beta")]:
                    self.s_x.inc(sy.dma_start(dst[:, :], io[key]), 16)

            @blk.gpsimd
            def _(g):
                for dst, key in [(self.sel8, "sel8"), (self.rwT, "rwT")]:
                    self.s_w.inc(g.dma_start(dst[:, :], io[key]), 16)

    def _ln(self, x1f, sqb, xnc):
        nc, io = self.nc, self.io
        with nc.Block() as blk:
            info = {}
            for ci in range(8):
                c0 = ci * 512
                sl = ci % 2

                @blk.sync
                def _(sy, ci=ci, c0=c0):
                    if ci >= 2:
                        sy.wait_ge(self.s_v.h, info[(ci - 2, "done")])
                    for k in range(DK):
                        e = sy.dma_start(x1f[:, k * 512:(k + 1) * 512],
                                         io["x1cat"][k * 128:(k + 1) * 128, c0:c0 + 512])
                        self.s_x.inc(e, 16)
                    info[(ci, "x")] = self.s_x.n

                @blk.scalar
                def _(s, ci=ci, c0=c0, sl=sl):
                    s.wait_ge(self.s_x.h, info[(ci, "x")])
                    for k in range(DK):
                        e = s.activation(sqb[:, k * 512:(k + 1) * 512],
                                         x1f[:, k * 512:(k + 1) * 512], AF.Square)
                    info[(ci, "sq")] = self.s_s.inc(e)

                @blk.tensor
                def _(t, ci=ci, c0=c0, sl=sl):
                    if ci == 0:
                        self.s_x.wait(t)
                        self.s_v.wait(t)
                    ibS, bkS = self.pool.acquire(t)
                    ibQ, bkQ = self.pool.acquire(t)
                    t.wait_ge(self.s_x.h, info[(ci, "x")])
                    for k in range(DK):
                        t.matmul(bkS[0:1, 0:512], self.ones_c[:, :],
                                 x1f[:, k * 512:(k + 1) * 512],
                                 start=(k == 0), stop=(k == DK - 1))
                    t.wait_ge(self.s_s.h, info[(ci, "sq")])
                    for k in range(DK):
                        mm = t.matmul(bkQ[0:1, 0:512], self.ones_cb[:, :],
                                      sqb[:, k * 512:(k + 1) * 512],
                                      start=(k == 0), stop=(k == DK - 1))
                    info[(ci, "mmq")] = self.s_pe.inc(mm)
                    info[(ci, "banks")] = (ibS, bkS, ibQ, bkQ)

                @blk.vector
                def _(v, ci=ci, c0=c0, sl=sl):
                    ibS, bkS, ibQ, bkQ = info[(ci, "banks")]
                    v.wait_ge(self.s_pe.h, info[(ci, "mmq")])
                    v.tensor_scalar(self.r_m[0:1, sl * 512:(sl + 1) * 512], bkS[0:1, 0:512],
                                    1.0 / D, None, ALU.mult)
                    self.s_ev.inc(v.tensor_scalar(self.r_v[0:1, sl * 512:(sl + 1) * 512],
                                                  bkQ[0:1, 0:512], 1.0 / D, None, ALU.mult))
                    self.pool.release(ibS)
                    self.pool.release(ibQ)
                    v.wait_ge(self.s_ev.h, self.s_ev.n)
                    v.tensor_tensor(self.r_m2[0:1, sl * 512:(sl + 1) * 512],
                                    self.r_m[0:1, sl * 512:(sl + 1) * 512],
                                    self.r_m[0:1, sl * 512:(sl + 1) * 512], ALU.mult)
                    self.vedge(v)
                    e = v.tensor_tensor(self.r_v[0:1, sl * 512:(sl + 1) * 512],
                                        self.r_v[0:1, sl * 512:(sl + 1) * 512],
                                        self.r_m2[0:1, sl * 512:(sl + 1) * 512], ALU.subtract)
                    info[(ci, "var")] = self.s_v.inc(e)

                @blk.scalar
                def _(s, ci=ci, sl=sl):
                    s.wait_ge(self.s_v.h, info[(ci, "var")])
                    self.s_s.inc(s.activation(self.r_m2[0:1, sl * 512:(sl + 1) * 512],
                                              self.r_v[0:1, sl * 512:(sl + 1) * 512],
                                              AF.Ln, bias=self.epsc[0:1, 0:1]))
                    s.wait_ge(self.s_s.h, self.s_s.n)
                    e = s.activation(self.r_rstd[0:1, sl * 512:(sl + 1) * 512],
                                     self.r_m2[0:1, sl * 512:(sl + 1) * 512],
                                     AF.Exp, scale=-0.5)
                    info[(ci, "rstd")] = self.s_s.inc(e)

                @blk.vector
                def _(v, ci=ci, sl=sl):
                    v.wait_ge(self.s_s.h, info[(ci, "rstd")])
                    e = v.scalar_tensor_tensor(self.r_b[0:1, sl * 512:(sl + 1) * 512],
                                               self.r_m[0:1, sl * 512:(sl + 1) * 512], -1.0,
                                               self.r_rstd[0:1, sl * 512:(sl + 1) * 512],
                                               ALU.mult, ALU.mult)
                    info[(ci, "brow")] = self.s_v.inc(e)

                @blk.tensor
                def _(t, ci=ci, sl=sl):
                    t.wait_ge(self.s_v.h, info[(ci, "brow")])
                    ibA, bkA = self.pool.acquire(t)
                    ibB, bkB = self.pool.acquire(t)
                    t.matmul(bkA[:, 0:512], self.ones_r[:, :],
                             self.r_rstd[0:1, sl * 512:(sl + 1) * 512], start=True, stop=True)
                    mm = t.matmul(bkB[:, 0:512], self.ones_r[:, :],
                                  self.r_b[0:1, sl * 512:(sl + 1) * 512], start=True, stop=True)
                    info[(ci, "bc")] = (self.s_pe.inc(mm), ibA, bkA, ibB, bkB)

                @blk.vector
                def _(v, ci=ci, c0=c0):
                    cnt, ibA, bkA, ibB, bkB = info[(ci, "bc")]
                    v.wait_ge(self.s_pe.h, cnt)
                    for k0 in range(0, DK, 2):
                        for k in (k0, k0 + 1):
                            v.tensor_tensor(self.tmp1[k % 2][:, 0:512],
                                            x1f[:, k * 512:(k + 1) * 512], bkA[:, 0:512],
                                            ALU.mult)
                        self.vedge(v)
                        for k in (k0, k0 + 1):
                            e = v.tensor_tensor(xnc[:, k * 512:(k + 1) * 512],
                                                self.tmp1[k % 2][:, 0:512], bkB[:, 0:512],
                                                ALU.add)
                        self.s_v.inc(e)
                    self.pool.release(ibA)
                    self.pool.release(ibB)
                    info[(ci, "done")] = self.s_v.n

                @blk.sync
                def _(sy, ci=ci, c0=c0):
                    sy.wait_ge(self.s_v.h, info[(ci, "done")])
                    for k in range(DK):
                        e = sy.dma_start(self.xn1_dram[k * 128:(k + 1) * 128, c0:c0 + 512],
                                         xnc[:, k * 512:(k + 1) * 512])
                        self.s_out.inc(e, 16)


    def _mamba(self, mb, upto=None):
        nc, io = self.nc, self.io
        info = {}
        ip_loads = {}
        op_loads = {}
        tmp2 = [mb["t2a"], mb["t2b"]]
        with nc.Block() as blk:
            @blk.sync
            def _(sy):
                for dst, key in [(mb["alog"], "alog"), (mb["dp"], "dp"),
                                 (mb["convw"], "convw"), (mb["convb"], "convb")]:
                    self.s_x.inc(sy.dma_start(dst[:, :], io[key]), 16)
                self.mcons_x = self.s_x.n

            @blk.gpsimd
            def _(g):
                for dst, key in [(mb["sel16"], "sel16"), (mb["xprojT"], "xprojT"),
                                 (mb["dtwT"], "dtwT"), (mb["dtb_row"], "dtb_row")]:
                    self.s_w.inc(g.dma_start(dst[:, :], io[key]), 16)
                self.mcons_w = self.s_w.n

            @blk.scalar
            def _(s):
                s.wait_ge(self.s_x.h, self.mcons_x)
                s.activation(mb["A"][:, :], mb["alog"][:, :], AF.Exp)
                self.sedge(s)

            @blk.vector
            def _(v):
                self.s_s.wait(v)
                self.s_v.inc(v.tensor_scalar(mb["A"][:, :], mb["A"][:, :],
                                             -1.0, None, ALU.mult))
                self.mA_done = self.s_v.n

            @blk.gpsimd
            def _(g):
                for gi in range(2):
                    ip_loads[gi] = self.s_w.inc(g.dma_start(
                        mb["wip"][gi % 2][:, :], io["inprojT"][gi % 32]), 16)
                for gi in range(2):
                    op_loads[gi] = self.s_w.inc(g.dma_start(
                        mb["wop"][gi % 2][:, :], io["outprojT"][gi % 8]), 16)

            @blk.gpsimd
            def _(g):
                for ch in range(NCH):
                    for gi in range(ch * 32 + 2, min((ch + 1) * 32 + 2, NCH * 32)):
                        g.wait_ge(self.s_f1.h, gi - 1)
                        ip_loads[gi] = self.s_w.inc(g.dma_start(
                            mb["wip"][gi % 2][:, :], io["inprojT"][gi % 32]), 16)
                    for gi in range(ch * 8 + 2, min((ch + 1) * 8 + 2, NCH * 8)):
                        g.wait_ge(self.s_f2.h, gi - 1)
                        op_loads[gi] = self.s_w.inc(g.dma_start(
                            mb["wop"][gi % 2][:, :], io["outprojT"][gi % 8]), 16)

            for ch in range(NCH):
                c0 = ch * CH
                sl = ch % 2
                self._mamba_chunk(blk, mb, info, ip_loads, op_loads, tmp2, ch, c0, sl)

    def _mamba_chunk(self, blk, mb, info, ip_loads, op_loads, tmp2, ch, c0, sl):
        io = self.io

        @blk.sync
        def _(sy):
            if ch >= 1:
                sy.wait_ge(self.s_v.h, info[(ch - 1, "vout_done")])
            for k in range(DK):
                e = sy.dma_start(mb["xn1ch"][:, k * CH:(k + 1) * CH],
                                 self.xn1_dram[k * 128:(k + 1) * 128, c0:c0 + CH])
                self.s_x.inc(e, 16)
            info[(ch, "xn1")] = self.s_x.n

        @blk.scalar
        def _(s):
            s.wait_ge(self.s_x.h, info[(ch, "xn1")])
            if ch >= 1:
                s.wait_ge(self.s_v.h, info[(ch - 1, "vout_done")])
            for k in range(DK):
                e = s.activation(mb["vout"][:, k * CH:(k + 1) * CH],
                                 mb["xn1ch"][:, k * CH:(k + 1) * CH], AF.Square)
            info[(ch, "sq")] = self.s_s.inc(e)

        @blk.tensor
        def _(t):
            if ch == 0:
                self.s_x.wait(t)
                t.wait_ge(self.s_v.h, self.mA_done)
                self.s_w.wait(t, self.mcons_w)
                self.s_w.wait(t, ip_loads[1])
            t.wait_ge(self.s_s.h, info[(ch, "sq")])
            ibQ, bkQ = self.pool.acquire(t)
            for k in range(DK):
                mm = t.matmul(bkQ[0:1, 0:CH], self.ones_cb[:, :],
                              mb["vout"][:, k * CH:(k + 1) * CH],
                              start=(k == 0), stop=(k == DK - 1))
            info[(ch, "mmq")] = (self.s_pe.inc(mm), ibQ, bkQ)

        @blk.vector
        def _(v):
            cnt, ibQ, bkQ = info[(ch, "mmq")]
            v.wait_ge(self.s_pe.h, cnt)
            e = v.tensor_scalar(self.r_v[0:1, sl * 512:sl * 512 + CH],
                                bkQ[0:1, 0:CH], 1.0 / D, None, ALU.mult)
            self.s_ev.inc(e)
            self.pool.release(ibQ)
            info[(ch, "ms")] = self.s_ev.n

        @blk.scalar
        def _(s):
            s.wait_ge(self.s_ev.h, info[(ch, "ms")])
            self.s_s.inc(s.activation(self.r_m2[0:1, sl * 512:sl * 512 + CH],
                                      self.r_v[0:1, sl * 512:sl * 512 + CH],
                                      AF.Ln, bias=self.epsc[0:1, 0:1]))
            s.wait_ge(self.s_s.h, self.s_s.n)
            e = s.activation(self.r_rstd[0:1, sl * 512:sl * 512 + CH],
                             self.r_m2[0:1, sl * 512:sl * 512 + CH], AF.Exp, scale=-0.5)
            info[(ch, "rstd")] = self.s_s.inc(e)

        @blk.tensor
        def _(t):
            t.wait_ge(self.s_s.h, info[(ch, "rstd")])
            ibA, bkA = self.pool.acquire(t)
            mm = t.matmul(bkA[:, 0:CH], self.ones_r[:, :],
                          self.r_rstd[0:1, sl * 512:sl * 512 + CH], start=True, stop=True)
            info[(ch, "rbc")] = (self.s_pe.inc(mm), ibA, bkA)

        @blk.vector
        def _(v):
            cnt, ibA, bkA = info[(ch, "rbc")]
            v.wait_ge(self.s_pe.h, cnt)
            for k0 in range(0, DK, 2):
                for k in (k0, k0 + 1):
                    v.tensor_tensor(tmp2[k % 2][:, 0:CH],
                                    mb["xn1ch"][:, k * CH:(k + 1) * CH],
                                    bkA[:, 0:CH], ALU.mult)
                self.vedge(v)
                for k in (k0, k0 + 1):
                    e = v.tensor_scalar(mb["xmv"][:, k * CH:(k + 1) * CH],
                                        tmp2[k % 2][:, 0:CH],
                                        self.normw[:, k:k + 1], None, ALU.mult)
                self.s_v.inc(e)
            self.pool.release(ibA)
            xv = mb["xcr"].ap().rearrange("p (e c) -> p e c", c=CH + 3)
            if ch == 0:
                e = v.memset(xv[:, :, 0:3], 0.0)
            else:
                e = v.tensor_copy(tmp2[0][:, 0:ET * 3], xv[:, :, CH:CH + 3])
                self.s_v.inc(e)
                v.wait_ge(self.s_v.h, self.s_v.n)
                e = v.tensor_copy(xv[:, :, 0:3],
                                  tmp2[0][:, 0:ET * 3]
                                  .rearrange("p (e c) -> p e c", c=3))
            info[(ch, "xm")] = self.s_v.inc(e)

        for eo in range(32):
            @blk.tensor
            def _(t, eo=eo):
                gi = ch * 32 + eo
                t.wait_ge(self.s_w.h, ip_loads[gi])
                t.wait_ge(self.s_v.h, info[(ch, "xm")])
                ib, bk = self.pool.acquire(t)
                for k in range(DK):
                    mm = t.matmul(bk[:, 0:CH],
                                  mb["wip"][gi % 2][:, k * 128:(k + 1) * 128],
                                  mb["xmv"][:, k * CH:(k + 1) * CH],
                                  start=(k == 0), stop=(k == DK - 1))
                info[(ch, "ip", eo)] = (self.s_pe.inc(mm), ib, bk)

            @blk.scalar
            def _(s, eo=eo):
                cnt, ib, bk = info[(ch, "ip", eo)]
                s.wait_ge(self.s_pe.h, cnt)
                if eo < ET:
                    xv = mb["xcr"][:, eo * (CH + 3): (eo + 1) * (CH + 3)]
                    e = s.copy(xv[:, 3:CH + 3], bk[:, 0:CH])
                else:
                    e = s.activation(mb["vout"][:, (eo - ET) * CH:(eo - ET + 1) * CH],
                                     bk[:, 0:CH], AF.Silu)
                self.s_ev.inc(e)
                self.pool.release(ib)
                self.s_f1.inc(s.copy(self.dummy[0:1, 2:3], self.dummy[0:1, 2:3]))
                info[(ch, "ipe", eo)] = self.s_ev.n

        for et in range(ET):
            @blk.vector
            def _(v, et=et):
                v.wait_ge(self.s_ev.h, info[(ch, "ipe", et)])
                if et >= 2:
                    v.wait_ge(self.s_s.h, info[(ch, "xc", et - 2)])
                xv = mb["xcr"][:, et * (CH + 3): (et + 1) * (CH + 3)]
                ca = tmp2[et % 2]
                v.tensor_scalar(ca[:, 0:CH], xv[:, 0:CH],
                                mb["convw"][:, et * 4:et * 4 + 1], None, ALU.mult)
                for tap in range(1, 4):
                    self.vedge(v)
                    e = v.scalar_tensor_tensor(ca[:, 0:CH], xv[:, tap:tap + CH],
                                               mb["convw"][:, et * 4 + tap:et * 4 + tap + 1],
                                               ca[:, 0:CH], ALU.mult, ALU.add)
                info[(ch, "cv", et)] = self.s_v.inc(e)

            @blk.scalar
            def _(s, et=et):
                s.wait_ge(self.s_v.h, info[(ch, "cv", et)])
                ca = tmp2[et % 2]
                e = s.activation(mb["xc"][:, et * CH:(et + 1) * CH], ca[:, 0:CH],
                                 AF.Silu, bias=mb["convb"][:, et:et + 1])
                info[(ch, "xc", et)] = self.s_s.inc(e)
        info[(ch, "xc_done")] = self.s_s.n

        @blk.tensor
        def _(t):
            t.wait_ge(self.s_s.h, info[(ch, "xc_done")])
            ibr, bkr = self.pool.acquire(t)
            ibbc, bkbc = self.pool.acquire(t)
            for et in range(ET):
                t.matmul(bkr[0:64, 0:CH],
                         mb["xprojT"][:, et * 96: et * 96 + 64],
                         mb["xc"][:, et * CH:(et + 1) * CH],
                         start=(et == 0), stop=(et == ET - 1))
            for et in range(ET):
                t.matmul(bkbc[0:16, 0:CH],
                         mb["xprojT"][:, et * 96 + 64: et * 96 + 80],
                         mb["xc"][:, et * CH:(et + 1) * CH],
                         start=(et == 0), stop=(et == ET - 1))
            for et in range(ET):
                mm = t.matmul(bkbc[32:48, 0:CH],
                              mb["xprojT"][:, et * 96 + 80: et * 96 + 96],
                              mb["xc"][:, et * CH:(et + 1) * CH],
                              start=(et == 0), stop=(et == ET - 1))
            info[(ch, "xp")] = (self.s_pe.inc(mm), ibr, bkr, ibbc, bkbc)

        @blk.scalar
        def _(s):
            cnt, ibr, bkr, ibbc, bkbc = info[(ch, "xp")]
            s.wait_ge(self.s_pe.h, cnt)
            s.copy(mb["xdbl"][0:64, 0:CH], bkr[0:64, 0:CH])
            s.copy(mb["Brow"][0:16, 0:CH], bkbc[0:16, 0:CH])
            e = s.copy(mb["Crow"][0:16, 0:CH], bkbc[32:48, 0:CH])
            self.s_ev.inc(e)
            self.sedge(s)
            self.pool.release(ibr)
            self.pool.release(ibbc)
            info[(ch, "xdbl")] = self.s_ev.n

        for half, buf in [(0, "Bb"), (1, "Cb")]:
            for n0 in range(0, NS, 4):
                bc_grp = {}

                @blk.tensor
                def _(t, half=half, buf=buf, n0=n0, bc_grp=bc_grp):
                    t.wait_ge(self.s_ev.h, info[(ch, "xdbl")])
                    for n in range(n0, n0 + 4):
                        ib, bk = self.pool.acquire(t)
                        src = mb["Brow"] if half == 0 else mb["Crow"]
                        mm = t.matmul(bk[:, 0:CH], mb["sel16"][:, n * 128:(n + 1) * 128],
                                      src[0:16, 0:CH], start=True, stop=True)
                        bc_grp[n] = (self.s_pe.inc(mm), ib, bk)

                @blk.scalar
                def _(s, half=half, buf=buf, n0=n0, bc_grp=bc_grp):
                    for n in range(n0, n0 + 4):
                        cnt, ib, bk = bc_grp[n]
                        s.wait_ge(self.s_pe.h, cnt)
                        e = s.copy(mb[buf][:, n * CH:(n + 1) * CH], bk[:, 0:CH])
                        self.s_ev.inc(e)
                        self.pool.release(ib)
        info[(ch, "bc_done")] = self.s_ev.n

        for et in range(ET):
            @blk.tensor
            def _(t, et=et):
                t.wait_ge(self.s_ev.h, info[(ch, "bc_done")])
                if et == 0 and ch > 0:
                    t.wait_ge(self.s_v.h, info[(ch - 1, "ybar_done")])
                ib, bk = self.pool.acquire(t)
                t.matmul(bk[:, 0:CH], mb["dtwT"][:, et * 128:(et + 1) * 128],
                         mb["xdbl"][0:64, 0:CH], start=True, stop=False)
                mm = t.matmul(bk[:, 0:CH], mb["dtb_row"][0:1, et * 128:(et + 1) * 128],
                              self.ones_rb[0:1, 0:CH], start=False, stop=True)
                info[(ch, "dtp", et)] = (self.s_pe.inc(mm), ib, bk)

            @blk.scalar
            def _(s, et=et):
                cnt, ib, bk = info[(ch, "dtp", et)]
                s.wait_ge(self.s_pe.h, cnt)
                if et >= 1:
                    s.wait_ge(self.s_v.h, info[(ch, "dt", et - 1)])
                e = s.activation(mb["dtu"][:, CH:2 * CH], bk[:, 0:CH], AF.Abs)
                info[(ch, "dtabs", et)] = self.s_s.inc(e)

            @blk.vector
            def _(v, et=et):
                cnt, ib, bk = info[(ch, "dtp", et)]
                v.wait_ge(self.s_s.h, info[(ch, "dtabs", et)])
                e = v.tensor_scalar(mb["y2"][:, 0:CH], bk[:, 0:CH], 0.0, None, ALU.max)
                self.s_ev.inc(e)
                self.pool.release(ib)
                info[(ch, "dtrelu", et)] = self.s_ev.n

            @blk.scalar
            def _(s, et=et):
                s.wait_ge(self.s_ev.h, info[(ch, "dtrelu", et)])
                self.s_s.inc(s.activation(mb["dtu"][:, CH:2 * CH], mb["dtu"][:, CH:2 * CH],
                                          AF.Exp, scale=-1.0))
                s.wait_ge(self.s_s.h, self.s_s.n)
                e = s.activation(mb["dtu"][:, CH:2 * CH], mb["dtu"][:, CH:2 * CH],
                                 AF.Ln, bias=self.ones_c[:, 0:1])
                info[(ch, "dtln", et)] = self.s_s.inc(e)

            @blk.vector
            def _(v, et=et):
                v.wait_ge(self.s_s.h, info[(ch, "dtln", et)])
                e = v.tensor_tensor(mb["dtu"][:, 0:CH], mb["y2"][:, 0:CH],
                                    mb["dtu"][:, CH:2 * CH], ALU.add)
                info[(ch, "dt", et)] = self.s_v.inc(e)
                v.wait_ge(self.s_v.h, self.s_v.n)
                e = v.tensor_tensor(mb["dtu"][:, CH:2 * CH], mb["dtu"][:, 0:CH],
                                    mb["xc"][:, et * CH:(et + 1) * CH], ALU.mult)
                info[(ch, "u", et)] = self.s_v.inc(e)

            for ng in range(NS // NG):
                @blk.scalar
                def _(s, et=et, ng=ng):
                    s.wait_ge(self.s_v.h, info[(ch, "dt", et)])
                    if (et, ng) != (0, 0) or ch > 0:
                        prev = info.get((ch, "scan", et, ng - 1)) or                             info.get((ch, "scan", et - 1, NS // NG - 1)) or                             info.get((ch - 1, "scan", ET - 1, NS // NG - 1))
                        if prev is not None:
                            s.wait_ge(self.s_v.h, prev)
                    for j in range(NG):
                        n = ng * NG + j
                        e = s.activation(mb["dA"][:, j * CH:(j + 1) * CH],
                                         mb["dtu"][:, 0:CH], AF.Exp,
                                         scale=mb["A"][:, et * NS + n: et * NS + n + 1])
                    info[(ch, "dA", et, ng)] = self.s_s.inc(e)

                @blk.vector
                def _(v, et=et, ng=ng):
                    v.wait_ge(self.s_s.h, info[(ch, "dA", et, ng)])
                    v.wait_ge(self.s_v.h, info[(ch, "u", et)])
                    uview = mb["dtu"][:, CH:2 * CH].unsqueeze(1).broadcast_to([128, NG, CH])
                    bview = mb["Bb"].ap().rearrange("p (n c) -> p n c", c=CH)[
                        :, ng * NG:(ng + 1) * NG, :]
                    e = v.tensor_tensor(mb["uB"].ap().rearrange("p (n c) -> p n c", c=CH),
                                        uview, bview, ALU.mult)
                    self.s_v.inc(e)
                    v.wait_ge(self.s_v.h, self.s_v.n)
                    for j in range(NG):
                        n = ng * NG + j
                        init = 0.0 if ch == 0 else self.carry_ap(mb, et, n)
                        e = v.tensor_tensor_scan(mb["hh"][:, j * CH:(j + 1) * CH],
                                                 mb["dA"][:, j * CH:(j + 1) * CH],
                                                 mb["uB"][:, j * CH:(j + 1) * CH],
                                                 init, ALU.mult, ALU.add)
                    self.s_v.inc(e)
                    v.wait_ge(self.s_v.h, self.s_v.n)
                    hv = mb["hh"].ap().rearrange("p (n c) -> p n c", c=CH)
                    e = v.tensor_copy(mb["carry"][:, et * NS + ng * NG:
                                                  et * NS + (ng + 1) * NG],
                                      hv[:, :, CH - 1:CH].squeeze(2))
                    self.s_v.inc(e)
                    cview = mb["Cb"].ap().rearrange("p (n c) -> p n c", c=CH)[
                        :, ng * NG:(ng + 1) * NG, :]
                    e = v.tensor_tensor(mb["uB"].ap().rearrange("p (n c) -> p n c", c=CH),
                                        mb["hh"].ap().rearrange("p (n c) -> p n c", c=CH),
                                        cview, ALU.mult)
                    self.s_v.inc(e)
                    v.wait_ge(self.s_v.h, self.s_v.n)
                    red = mb["uB"].ap().rearrange("p (n c) -> p c n", c=CH)
                    dst = mb["y2"][:, 0:CH] if ng > 0 else mb["yy"][:, 0:CH]
                    e = v.tensor_reduce(dst, red, AX.X, ALU.add)
                    self.s_v.inc(e)
                    if ng > 0:
                        v.wait_ge(self.s_v.h, self.s_v.n)
                        e = v.tensor_tensor(mb["yy"][:, 0:CH], mb["yy"][:, 0:CH],
                                            mb["y2"][:, 0:CH], ALU.add)
                        self.s_v.inc(e)
                    info[(ch, "scan", et, ng)] = self.s_v.n

            @blk.vector
            def _(v, et=et):
                v.wait_ge(self.s_v.h, info[(ch, "scan", et, NS // NG - 1)])
                e = v.scalar_tensor_tensor(mb["yy"][:, 0:CH],
                                           mb["xc"][:, et * CH:(et + 1) * CH],
                                           mb["dp"][:, et:et + 1],
                                           mb["yy"][:, 0:CH], ALU.mult, ALU.add)
                self.s_v.inc(e)
                v.wait_ge(self.s_v.h, self.s_v.n)
                e = v.tensor_tensor(mb["xcr"][:, et * (CH + 3): et * (CH + 3) + CH],
                                    mb["yy"][:, 0:CH],
                                    mb["vout"][:, et * CH:(et + 1) * CH], ALU.mult)
                info[(ch, "ybar", et)] = self.s_v.inc(e)
        info[(ch, "ybar_done")] = self.s_v.n

        for do in range(DK):
            @blk.tensor
            def _(t, do=do):
                gi = ch * 8 + do
                t.wait_ge(self.s_w.h, op_loads[gi])
                t.wait_ge(self.s_v.h, info[(ch, "ybar_done")])
                ib, bk = self.pool.acquire(t)
                for et in range(ET):
                    mm = t.matmul(bk[:, 0:CH],
                                  mb["wop"][gi % 2][:, et * 128:(et + 1) * 128],
                                  mb["xcr"][:, et * (CH + 3): et * (CH + 3) + CH],
                                  start=(et == 0), stop=(et == ET - 1))
                info[(ch, "op", do)] = (self.s_pe.inc(mm), ib, bk)

            @blk.vector
            def _(v, do=do):
                cnt, ib, bk = info[(ch, "op", do)]
                v.wait_ge(self.s_pe.h, cnt)
                e = v.tensor_tensor(mb["vout"][:, do * CH:(do + 1) * CH], bk[:, 0:CH],
                                    mb["xn1ch"][:, do * CH:(do + 1) * CH], ALU.add)
                self.s_ev.inc(e)
                self.pool.release(ib)
                if do == DK - 1:
                    info[(ch, "v_done")] = self.s_ev.n
                self.s_f2.inc(v.tensor_copy(self.dummy[0:1, 3:4], self.dummy[0:1, 3:4]))

        @blk.scalar
        def _(s):
            s.wait_ge(self.s_ev.h, info[(ch, "v_done")])
            for k in range(DK):
                e = s.activation(mb["xcr"][:, k * (CH + 3): k * (CH + 3) + CH],
                                 mb["vout"][:, k * CH:(k + 1) * CH], AF.Square)
            info[(ch, "sq2")] = self.s_s.inc(e)

        @blk.tensor
        def _(t):
            t.wait_ge(self.s_s.h, info[(ch, "sq2")])
            ibQ, bkQ = self.pool.acquire(t)
            for k in range(DK):
                mm = t.matmul(bkQ[0:1, 0:CH], self.ones_cb[:, :],
                              mb["xcr"][:, k * (CH + 3): k * (CH + 3) + CH],
                              start=(k == 0), stop=(k == DK - 1))
            info[(ch, "mq2")] = (self.s_pe.inc(mm), ibQ, bkQ)

        @blk.vector
        def _(v):
            cnt, ibQ, bkQ = info[(ch, "mq2")]
            v.wait_ge(self.s_pe.h, cnt)
            e = v.tensor_scalar(self.r_v[0:1, sl * 512:sl * 512 + CH], bkQ[0:1, 0:CH],
                                1.0 / D, None, ALU.mult)
            self.s_ev.inc(e)
            self.pool.release(ibQ)
            info[(ch, "ms2")] = self.s_ev.n

        @blk.scalar
        def _(s):
            s.wait_ge(self.s_ev.h, info[(ch, "ms2")])
            self.s_s.inc(s.activation(self.r_m2[0:1, sl * 512:sl * 512 + CH],
                                      self.r_v[0:1, sl * 512:sl * 512 + CH],
                                      AF.Ln, bias=self.epsc[0:1, 0:1]))
            s.wait_ge(self.s_s.h, self.s_s.n)
            e = s.activation(self.r_rstd[0:1, sl * 512:sl * 512 + CH],
                             self.r_m2[0:1, sl * 512:sl * 512 + CH], AF.Exp, scale=-0.5)
            info[(ch, "rstd2")] = self.s_s.inc(e)

        @blk.tensor
        def _(t):
            t.wait_ge(self.s_s.h, info[(ch, "rstd2")])
            ibA, bkA = self.pool.acquire(t)
            mm = t.matmul(bkA[:, 0:CH], self.ones_r[:, :],
                          self.r_rstd[0:1, sl * 512:sl * 512 + CH], start=True, stop=True)
            info[(ch, "rbc2")] = (self.s_pe.inc(mm), ibA, bkA)

        info[(ch, "ob")] = self.s_out.n

        @blk.vector
        def _(v):
            cnt, ibA, bkA = info[(ch, "rbc2")]
            v.wait_ge(self.s_pe.h, cnt)
            for k in range(DK):
                v.tensor_tensor(mb["ot"][:, 0:CH], mb["vout"][:, k * CH:(k + 1) * CH],
                                bkA[:, 0:CH], ALU.mult)
                self.vedge(v)
                e = v.tensor_scalar(mb["ot"][:, 0:CH], mb["ot"][:, 0:CH],
                                    self.normf[:, k:k + 1], None, ALU.mult)
                self.s_v.inc(e)
                v.wait_ge(self.s_v.h, self.s_v.n)
                if k >= 2:
                    v.wait_ge(self.s_out.h, info[(ch, "ob")] + 16 * (k - 1))
                slq = (k % 2) * CH
                if c0 < TC:
                    e = v.tensor_copy(mb["otr"][:, slq:slq + CH], mb["ot"][:, 0:CH])
                else:
                    e = v.tensor_copy(mb["otr"][:, slq:slq + CH],
                                      mb["ot"][:, 0:CH][:, ::-1])
                info[(ch, "otr", k)] = self.s_v.inc(e)
            self.pool.release(ibA)
            info[(ch, "vout_done")] = self.s_v.n

        @blk.sync
        def _(sy):
            if c0 < TC:
                dst, j0 = self.oacc, c0
            else:
                dst, j0 = io["contrib"], 3584 - c0
            for k in range(DK):
                sy.wait_ge(self.s_v.h, info[(ch, "otr", k)])
                slq = (k % 2) * CH
                e = sy.dma_start(dst[k * 128:(k + 1) * 128, j0:j0 + CH],
                                 mb["otr"][:, slq:slq + CH])
                self.s_out.inc(e, 16)

    def carry_ap(self, mb, et, n):
        return mb["carry"][:, et * NS + n: et * NS + n + 1]


    def _ffn(self, fb):
        nc, io = self.nc, self.io
        self.f1_base = self.s_f1.n
        self.f2_base = self.s_f2.n
        w1_loads = {}
        w2_loads = {}
        order = [(th, e) for th in range(2) for e in range(NE)]

        def w1_key(gi):
            oi2, band = gi // 16, gi % 16
            th2, e2 = order[oi2]
            return (th2, e2, band), e2, band

        def w2_key(gi):
            oi2, db = gi // 8, gi % 8
            th2, e2 = order[oi2]
            return (th2, e2, db), e2, db

        NW1 = len(order) * 16
        NW2 = len(order) * 8

        for th in range(2):
            with nc.Block() as blk:
                if th == 0:
                    @blk.gpsimd
                    def _(g):
                        for gi in range(3):
                            key, e2, band = w1_key(gi)
                            w1_loads[key] = self.s_w.inc(g.dma_start(
                                fb["w1sb"][gi % 3][:, :], io["fw1"][e2, band]), 16)
                        for gi in range(2):
                            key, e2, db = w2_key(gi)
                            w2_loads[key] = self.s_w.inc(g.dma_start(
                                fb["w2sb"][gi % 2][:, :], io["fw2"][e2, db]), 16)

                info = {}
                for tc2 in range(2):
                    c0 = th * 1024 + tc2 * 512
                    sl = tc2 % 2

                    @blk.sync
                    def _(sy, tc2=tc2, c0=c0):
                        if th > 0 or tc2 > 0:
                            sy.wait_ge(self.s_v.h, self.s_v.n)
                        for k in range(DK):
                            e = sy.dma_start(
                                fb["xnin"][:, k * 512:(k + 1) * 512],
                                self.xn1_dram[k * 128:(k + 1) * 128, c0:c0 + 512])
                            self.s_x.inc(e, 16)
                        info[(tc2, "x")] = self.s_x.n

                    @blk.scalar
                    def _(s, tc2=tc2, sl=sl):
                        s.wait_ge(self.s_x.h, info[(tc2, "x")])
                        for k in range(DK):
                            e = s.activation(fb["sqf"][:, k * 512:(k + 1) * 512],
                                             fb["xnin"][:, k * 512:(k + 1) * 512],
                                             AF.Square)
                        info[(tc2, "sq")] = self.s_s.inc(e)

                    @blk.tensor
                    def _(t, tc2=tc2, sl=sl):
                        t.wait_ge(self.s_s.h, info[(tc2, "sq")])
                        ibS, bkS = self.pool.acquire(t)
                        ibQ, bkQ = self.pool.acquire(t)
                        for k in range(DK):
                            t.matmul(bkS[0:1, 0:512], self.ones_cb[:, :],
                                     fb["xnin"][:, k * 512:(k + 1) * 512],
                                     start=(k == 0), stop=(k == DK - 1))
                        for k in range(DK):
                            mm = t.matmul(bkQ[0:1, 0:512], self.ones_cb[:, :],
                                          fb["sqf"][:, k * 512:(k + 1) * 512],
                                          start=(k == 0), stop=(k == DK - 1))
                        info[(tc2, "mm")] = (self.s_pe.inc(mm), ibS, bkS, ibQ, bkQ)

                    @blk.vector
                    def _(v, tc2=tc2, sl=sl):
                        cnt, ibS, bkS, ibQ, bkQ = info[(tc2, "mm")]
                        v.wait_ge(self.s_pe.h, cnt)
                        v.tensor_scalar(self.r_m[0:1, sl * 512:(sl + 1) * 512],
                                        bkS[0:1, 0:512], 1.0 / D, None, ALU.mult)
                        self.s_ev.inc(v.tensor_scalar(self.r_v[0:1, sl * 512:(sl + 1) * 512],
                                                      bkQ[0:1, 0:512], 1.0 / D, None,
                                                      ALU.mult))
                        self.pool.release(ibS)
                        self.pool.release(ibQ)
                        v.wait_ge(self.s_ev.h, self.s_ev.n)
                        v.tensor_tensor(self.r_m2[0:1, sl * 512:(sl + 1) * 512],
                                        self.r_m[0:1, sl * 512:(sl + 1) * 512],
                                        self.r_m[0:1, sl * 512:(sl + 1) * 512], ALU.mult)
                        self.vedge(v)
                        e = v.tensor_tensor(self.r_v[0:1, sl * 512:(sl + 1) * 512],
                                            self.r_v[0:1, sl * 512:(sl + 1) * 512],
                                            self.r_m2[0:1, sl * 512:(sl + 1) * 512],
                                            ALU.subtract)
                        info[(tc2, "var")] = self.s_v.inc(e)

                    @blk.scalar
                    def _(s, tc2=tc2, sl=sl):
                        s.wait_ge(self.s_v.h, info[(tc2, "var")])
                        self.s_s.inc(s.activation(self.r_m2[0:1, sl * 512:(sl + 1) * 512],
                                                  self.r_v[0:1, sl * 512:(sl + 1) * 512],
                                                  AF.Ln, bias=self.epsc[0:1, 0:1]))
                        s.wait_ge(self.s_s.h, self.s_s.n)
                        e = s.activation(self.r_rstd[0:1, sl * 512:(sl + 1) * 512],
                                         self.r_m2[0:1, sl * 512:(sl + 1) * 512],
                                         AF.Exp, scale=-0.5)
                        info[(tc2, "rstd")] = self.s_s.inc(e)

                    @blk.vector
                    def _(v, tc2=tc2, sl=sl):
                        v.wait_ge(self.s_s.h, info[(tc2, "rstd")])
                        e = v.scalar_tensor_tensor(self.r_b[0:1, sl * 512:(sl + 1) * 512],
                                                   self.r_m[0:1, sl * 512:(sl + 1) * 512],
                                                   -1.0,
                                                   self.r_rstd[0:1, sl * 512:(sl + 1) * 512],
                                                   ALU.mult, ALU.mult)
                        info[(tc2, "brow")] = self.s_v.inc(e)

                    @blk.tensor
                    def _(t, tc2=tc2, sl=sl):
                        t.wait_ge(self.s_v.h, info[(tc2, "brow")])
                        ibA, bkA = self.pool.acquire(t)
                        ibB, bkB = self.pool.acquire(t)
                        t.matmul(bkA[:, 0:512], self.ones_r[:, :],
                                 self.r_rstd[0:1, sl * 512:(sl + 1) * 512],
                                 start=True, stop=True)
                        mm = t.matmul(bkB[:, 0:512], self.ones_r[:, :],
                                      self.r_b[0:1, sl * 512:(sl + 1) * 512],
                                      start=True, stop=True)
                        info[(tc2, "bc")] = (self.s_pe.inc(mm), ibA, bkA, ibB, bkB)

                    @blk.vector
                    def _(v, tc2=tc2):
                        cnt, ibA, bkA, ibB, bkB = info[(tc2, "bc")]
                        v.wait_ge(self.s_pe.h, cnt)
                        for k0 in range(0, DK, 2):
                            for k in (k0, k0 + 1):
                                v.tensor_tensor(fb["sqf"][:, k * 512:(k + 1) * 512],
                                                fb["xnin"][:, k * 512:(k + 1) * 512],
                                                bkA[:, 0:512], ALU.mult)
                            self.vedge(v)
                            for k in (k0, k0 + 1):
                                e = v.tensor_tensor(fb["sqf"][:, k * 512:(k + 1) * 512],
                                                    fb["sqf"][:, k * 512:(k + 1) * 512],
                                                    bkB[:, 0:512], ALU.add)
                            self.s_v.inc(e)
                            self.vedge(v)
                            for k in (k0, k0 + 1):
                                e = v.tensor_scalar(
                                    fb["xnf"][:, k * 1024 + tc2 * 512:
                                              k * 1024 + tc2 * 512 + 512],
                                    fb["sqf"][:, k * 512:(k + 1) * 512],
                                    self.gamma[:, k:k + 1], self.beta[:, k:k + 1],
                                    ALU.mult, ALU.add)
                            self.s_v.inc(e)
                        self.pool.release(ibA)
                        self.pool.release(ibB)
                        info[(tc2, "xnf")] = self.s_v.n

                    @blk.sync
                    def _(sy, tc2=tc2, c0=c0):
                        x1b = fb["h_sb"].ap().bitcast(F32)
                        for k in range(DK):
                            e = sy.dma_start(x1b[:, k * 512:(k + 1) * 512],
                                             io["x1cat"][k * 128:(k + 1) * 128,
                                                         c0:c0 + 512])
                            self.s_x.inc(e, 16)
                            e = sy.dma_start(x1b[:, 4096 + k * 512: 4096 + (k + 1) * 512],
                                             self.oacc[k * 128:(k + 1) * 128, c0:c0 + 512])
                            self.s_x.inc(e, 16)
                        info[(tc2, "xinit")] = self.s_x.n

                    @blk.vector
                    def _(v, tc2=tc2, c0=c0):
                        v.wait_ge(self.s_x.h, info[(tc2, "xinit")])
                        v.wait_ge(self.s_v.h, info[(tc2, "xnf")])
                        x1b = fb["h_sb"].ap().bitcast(F32)
                        for k0 in range(0, DK, 2):
                            for k in (k0, k0 + 1):
                                v.tensor_tensor(self.tmp1[k % 2][:, 0:512],
                                                x1b[:, k * 512:(k + 1) * 512],
                                                x1b[:, 4096 + k * 512:
                                                    4096 + (k + 1) * 512], ALU.add)
                            self.vedge(v)
                            for k in (k0, k0 + 1):
                                e = v.tensor_tensor(self.acc[:, k * TC + c0:
                                                             k * TC + c0 + 512],
                                                    self.tmp1[k % 2][:, 0:512],
                                                    fb["xnin"][:, k * 512:(k + 1) * 512],
                                                    ALU.add)
                            self.s_v.inc(e)
                        info[(tc2, "init")] = self.s_v.n
                self.xnf_done = info[(1, "xnf")]

                rinfo = {}

                @blk.tensor
                def _(t, th=th):
                    t.wait_ge(self.s_v.h, self.xnf_done)
                    for tc2 in range(2):
                        ib, bk = self.pool.acquire(t)
                        for k in range(DK):
                            mm = t.matmul(bk[0:8, 0:512], self.rwT[:, k * 8:(k + 1) * 8],
                                          fb["xnf"][:, k * 1024 + tc2 * 512:
                                                    k * 1024 + tc2 * 512 + 512],
                                          start=(k == 0), stop=(k == DK - 1))
                        rinfo[(tc2, "sc")] = (self.s_pe.inc(mm), ib, bk)

                @blk.scalar
                def _(s, th=th):
                    for tc2 in range(2):
                        cnt, ib, bk = rinfo[(tc2, "sc")]
                        s.wait_ge(self.s_pe.h, cnt)
                        e = s.copy(fb["scT"][0:8, th * 1024 + tc2 * 512:
                                             th * 1024 + tc2 * 512 + 512], bk[0:8, 0:512])
                        self.s_ev.inc(e)
                        self.pool.release(ib)
                    rinfo["scT"] = self.s_ev.n

                et_base = self.s_et.n

                @blk.tensor
                def _(t, th=th):
                    t.wait_ge(self.s_ev.h, rinfo["scT"])
                    for tt in range(8):
                        if tt >= 1:
                            t.wait_ge(self.s_et.h, et_base + tt)
                        mm = t.transpose(self.ptr_bf[:, 0:8],
                                         fb["scT"][0:8, th * 1024 + tt * 128:
                                                   th * 1024 + (tt + 1) * 128],
                                         self.ident[0:8, 0:8])
                        rinfo[(tt, "tr")] = self.s_pe.inc(mm)

                @blk.vector
                def _(v, th=th):
                    for tt in range(8):
                        v.wait_ge(self.s_pe.h, rinfo[(tt, "tr")])
                        self.s_et.inc(v.tensor_copy(fb["stok"][:, tt * 8:(tt + 1) * 8],
                                                    self.ptr_bf[:, 0:8]))
                    rinfo["stok"] = self.s_et.n

                @blk.vector
                def _(v, th=th):
                    v.wait_ge(self.s_et.h, rinfo["stok"])
                    for tt in range(8):
                        e = v.max(fb["top8"][:, tt * 8:(tt + 1) * 8],
                                  fb["stok"][:, tt * 8:(tt + 1) * 8])
                    self.s_v.inc(e)
                    v.wait_ge(self.s_v.h, self.s_v.n)
                    for tt in range(8):
                        v.tensor_tensor(fb["keep"][:, tt * 8:(tt + 1) * 8],
                                        fb["stok"][:, tt * 8:(tt + 1) * 8],
                                        fb["top8"][:, tt * 8 + 1:tt * 8 + 2]
                                        .broadcast_to([128, 8]), ALU.is_ge)
                        e = v.tensor_scalar(fb["negm1"][:, tt:tt + 1],
                                            fb["top8"][:, tt * 8:tt * 8 + 1],
                                            -1.0 / 1.5, None, ALU.mult)
                    rinfo["nm1"] = self.s_v.inc(e)

                @blk.scalar
                def _(s, th=th):
                    s.wait_ge(self.s_v.h, rinfo["nm1"])
                    for tt in range(8):
                        e = s.activation(fb["ew"][:, tt * 8:(tt + 1) * 8],
                                         fb["stok"][:, tt * 8:(tt + 1) * 8], AF.Exp,
                                         scale=1.0 / 1.5, bias=fb["negm1"][:, tt:tt + 1])
                    rinfo["ew"] = self.s_s.inc(e)

                @blk.vector
                def _(v, th=th):
                    v.wait_ge(self.s_s.h, rinfo["ew"])
                    for tt in range(8):
                        e = v.reduce_sum(fb["wsum"][:, tt:tt + 1],
                                         fb["ew"][:, tt * 8:(tt + 1) * 8], AX.X)
                    self.s_v.inc(e)
                    v.wait_ge(self.s_v.h, self.s_v.n)
                    for tt in range(8):
                        e = v.reciprocal(fb["wrec"][:, tt:tt + 1], fb["wsum"][:, tt:tt + 1])
                    self.s_v.inc(e)
                    v.wait_ge(self.s_v.h, self.s_v.n)
                    for tt in range(8):
                        e = v.tensor_tensor(fb["wnorm"][:, tt * 8:(tt + 1) * 8],
                                            fb["ew"][:, tt * 8:(tt + 1) * 8],
                                            fb["wrec"][:, tt:tt + 1].broadcast_to([128, 8]),
                                            ALU.mult)
                    self.s_v.inc(e)
                    v.wait_ge(self.s_v.h, self.s_v.n)
                    for tt in range(8):
                        e = v.tensor_tensor(fb["wnorm"][:, tt * 8:(tt + 1) * 8],
                                            fb["wnorm"][:, tt * 8:(tt + 1) * 8],
                                            fb["keep"][:, tt * 8:(tt + 1) * 8], ALU.mult)
                    rinfo["wn"] = self.s_v.inc(e)

                wet = self.s_et.n

                @blk.tensor
                def _(t, th=th):
                    t.wait_ge(self.s_v.h, rinfo["wn"])
                    for tt in range(8):
                        if tt >= 1:
                            t.wait_ge(self.s_et.h, wet + tt)
                        mm = t.transpose(self.ptr_f[0:8, 0:128],
                                         fb["wnorm"][:, tt * 8:(tt + 1) * 8],
                                         self.identf[:, :])
                        rinfo[(tt, "wtr")] = self.s_pe.inc(mm)

                @blk.scalar
                def _(s, th=th):
                    for tt in range(8):
                        s.wait_ge(self.s_pe.h, rinfo[(tt, "wtr")])
                        e = s.copy(self.wT_bf[0:8, th * 1024 + tt * 128:
                                              th * 1024 + (tt + 1) * 128],
                                   self.ptr_f[0:8, 0:128])
                        self.s_et.inc(e)
                self.wT_done = self.s_et.n

            for ei in range(NE):
                oi = th * NE + ei
                e = ei
                with nc.Block() as blk:
                    @blk.gpsimd
                    def _(g, oi=oi):
                        for gi in range(oi * 16 + 3, min((oi + 1) * 16 + 3, NW1)):
                            g.wait_ge(self.s_f1.h, self.f1_base + gi - 2)
                            key, e2, band = w1_key(gi)
                            w1_loads[key] = self.s_w.inc(g.dma_start(
                                fb["w1sb"][gi % 3][:, :], io["fw1"][e2, band]), 16)
                        for gi in range(oi * 8 + 2, min((oi + 1) * 8 + 2, NW2)):
                            g.wait_ge(self.s_f2.h, self.f2_base + gi - 1)
                            key, e2, db = w2_key(gi)
                            w2_loads[key] = self.s_w.inc(g.dma_start(
                                fb["w2sb"][gi % 2][:, :], io["fw2"][e2, db]), 16)

                    wb_cnt = {}

                    @blk.tensor
                    def _(t, th=th, e=e, oi=oi):
                        t.wait_ge(self.s_et.h, self.wT_done)
                        for tci in range(2):
                            c0 = th * 1024 + tci * 512
                            ib, bk = self.pool.acquire(t)
                            mm = t.matmul(bk[:, 0:512],
                                          self.sel8[0:8, e * 128:(e + 1) * 128],
                                          self.wT_bf[0:8, c0:c0 + 512],
                                          start=True, stop=True)
                            wb_cnt[tci] = (self.s_pe.inc(mm), ib, bk)

                    @blk.scalar
                    def _(s, th=th, e=e):
                        for tci in range(2):
                            cnt, ib, bk = wb_cnt[tci]
                            s.wait_ge(self.s_pe.h, cnt)
                            e2 = s.copy(fb["wbc"][:, tci * 512:(tci + 1) * 512],
                                        bk[:, 0:512])
                            self.s_ev.inc(e2)
                            self.pool.release(ib)
                        self.wbc_done = self.s_ev.n

                    @blk.tensor
                    def _(t, oi=oi):
                        t.wait_ge(self.s_ev.h, self.wbc_done)
                        if oi >= 1:
                            t.wait_ge(self.s_pe.h, self.w2_mm_done)

                    for m in range(32):
                        h_cnt = {}

                        @blk.tensor
                        def _(t, th=th, e=e, oi=oi, m=m, h_cnt=h_cnt):
                            band = m // 2
                            idx = oi * 16 + band
                            t.wait_ge(self.s_w.h, w1_loads[(th, e, band)])
                            wsb = fb["w1sb"][idx % 3]
                            for tci in range(2):
                                ib, bk = self.pool.acquire(t)
                                for k in range(DK):
                                    mm = t.matmul(
                                        bk[:, 0:512],
                                        wsb[:, k * 256 + (m % 2) * 128:
                                            k * 256 + (m % 2) * 128 + 128],
                                        fb["xnf"][:, k * 1024 + tci * 512:
                                                  k * 1024 + (tci + 1) * 512],
                                        start=(k == 0), stop=(k == DK - 1))
                                h_cnt[tci] = (self.s_pe.inc(mm), ib, bk)

                        @blk.scalar
                        def _(s, th=th, e=e, m=m, h_cnt=h_cnt):
                            for tci in range(2):
                                cnt, ib, bk = h_cnt[tci]
                                s.wait_ge(self.s_pe.h, cnt)
                                e2 = s.activation(fb["h_sb"][:, m * 1024 + tci * 512:
                                                             m * 1024 + (tci + 1) * 512],
                                                  bk[:, 0:512], AF.Silu)
                                h_cnt[(tci, "ev")] = self.s_ev.inc(e2)
                                self.pool.release(ib)
                            if m % 2 == 1:
                                self.s_f1.inc(s.copy(self.dummy[0:1, 2:3],
                                                     self.dummy[0:1, 2:3]))

                        @blk.vector
                        def _(v, th=th, e=e, m=m, h_cnt=h_cnt):
                            for tci in range(2):
                                v.wait_ge(self.s_ev.h, h_cnt[(tci, "ev")])
                                slh = fb["h_sb"][:, m * 1024 + tci * 512:
                                                 m * 1024 + (tci + 1) * 512]
                                e2 = v.tensor_tensor(slh, slh,
                                                     fb["wbc"][:, tci * 512:(tci + 1) * 512],
                                                     ALU.mult)
                                self.hw_done = self.s_v.inc(e2)

                    for db in range(8):
                        y_cnt = {}

                        @blk.tensor
                        def _(t, th=th, e=e, oi=oi, db=db, y_cnt=y_cnt):
                            if db == 0:
                                t.wait_ge(self.s_v.h, self.hw_done)
                            t.wait_ge(self.s_w.h, w2_loads[(th, e, db)])
                            wsb = fb["w2sb"][(oi * 8 + db) % 2]
                            for tci in range(2):
                                ib, bk = self.pool.acquire(t)
                                for m in range(32):
                                    mm = t.matmul(bk[:, 0:512],
                                                  wsb[:, m * 128:(m + 1) * 128],
                                                  fb["h_sb"][:, m * 1024 + tci * 512:
                                                             m * 1024 + (tci + 1) * 512],
                                                  start=(m == 0), stop=(m == 31))
                                y_cnt[tci] = (self.s_pe.inc(mm), ib, bk)
                            self.w2_mm_done = self.s_pe.n

                        @blk.vector
                        def _(v, th=th, e=e, db=db, y_cnt=y_cnt):
                            for tci in range(2):
                                cnt, ib, bk = y_cnt[tci]
                                v.wait_ge(self.s_pe.h, cnt)
                                c0 = th * 1024 + tci * 512
                                slh = self.acc[:, db * TC + c0: db * TC + c0 + 512]
                                e2 = v.tensor_tensor(slh, slh, bk[:, 0:512], ALU.add)
                                self.s_ev.inc(e2)
                                self.pool.release(ib)
                            self.s_f2.inc(v.tensor_copy(self.dummy[0:1, 3:4],
                                                        self.dummy[0:1, 3:4]))
                            self.moe_acc_done = self.s_ev.n

    def _store(self, skip_contrib, have_acc=True):
        with self.nc.Block() as blk:
            @blk.sync
            def _(sy):
                for s in [self.s_ev, self.s_v, self.s_s, self.s_x, self.s_out]:
                    s.wait(sy)
                if have_acc:
                    for k in range(DK):
                        self.s_out.inc(sy.dma_start(
                            self.io["x2p"][k * 128:(k + 1) * 128, :],
                            self.acc[:, k * TC:(k + 1) * TC]), 16)
                sy.wait_ge(self.s_out.h, self.s_out.n)


def _prep_l1_core(inputs, c, x1_parts):
    b, g = c & 3, c >> 2
    pre = "fwd" if g == 0 else "bwd"
    x1full = np.hstack([x1_parts[b], x1_parts[b + 4]])
    if g == 1:
        x1full = x1full[:, ::-1]
    x1cat = np.ascontiguousarray(x1full, dtype=np.float32)

    def col8(v):
        return np.ascontiguousarray(np.asarray(v, np.float32).reshape(8, 128).T)

    alog = np.asarray(inputs[f"L1_attn_{pre}_Alog"], np.float32)
    alog_dev = np.ascontiguousarray(
        alog.reshape(16, 128, 16).transpose(1, 0, 2).reshape(128, 256))
    dp = np.ascontiguousarray(
        np.asarray(inputs[f"L1_attn_{pre}_Dp"], np.float32).reshape(16, 128).T)
    convw = np.asarray(inputs[f"L1_attn_{pre}_convw"], np.float32)
    convw_dev = np.ascontiguousarray(
        convw.reshape(16, 128, 4).transpose(1, 0, 2).reshape(128, 64))
    convb = np.ascontiguousarray(
        np.asarray(inputs[f"L1_attn_{pre}_convb"], np.float32).reshape(16, 128).T)
    dtb_row = np.ascontiguousarray(
        np.asarray(inputs[f"L1_attn_{pre}_dtb"], np.float32).reshape(1, 2048))
    rw = np.asarray(inputs["L1_ffn_router_w"], np.float32)
    rwT = np.ascontiguousarray(rw.T.reshape(8, 128, 8).transpose(1, 0, 2).reshape(128, 64))
    xproj = np.asarray(inputs[f"L1_attn_{pre}_xproj"], np.float32)
    xprojT = np.ascontiguousarray(
        xproj.T.reshape(16, 128, 96).transpose(1, 0, 2).reshape(128, 16 * 96))
    dtw = np.asarray(inputs[f"L1_attn_{pre}_dtw"], np.float32)
    dtwT = np.ascontiguousarray(dtw.T)
    inproj = np.asarray(inputs[f"L1_attn_{pre}_inproj"], np.float32)
    ipb = inproj.reshape(32, 128, 8, 128).transpose(0, 3, 2, 1)
    ipb = np.ascontiguousarray(ipb.reshape(32, 128, 1024))
    outproj = np.asarray(inputs[f"L1_attn_{pre}_outproj"], np.float32)
    opb = outproj.reshape(8, 128, 16, 128).transpose(0, 3, 2, 1)
    opb = np.ascontiguousarray(opb.reshape(8, 128, 2048))
    FW1 = np.asarray(inputs["L1_ffn_W1"], np.float32)
    fw1 = FW1.reshape(8, 8, 128, 16, 256).transpose(0, 3, 2, 1, 4)
    fw1 = np.ascontiguousarray(fw1.reshape(8, 16, 128, 8 * 256))
    FW2 = np.asarray(inputs["L1_ffn_W2"], np.float32)
    fw2 = FW2.reshape(8, 32, 128, 8, 128).transpose(0, 3, 2, 1, 4)
    fw2 = np.ascontiguousarray(fw2.reshape(8, 8, 128, 32 * 128))
    sel16 = np.zeros((16, 16 * 128), np.float32)
    for n in range(16):
        sel16[n, n * 128:(n + 1) * 128] = 1.0
    sel8 = np.zeros((8, 8 * 128), np.float32)
    for e in range(8):
        sel8[e, e * 128:(e + 1) * 128] = 1.0
    return dict(
        x1cat=x1cat,
        ident=np.eye(128, dtype=np.float32).astype(BF),
        identf=np.eye(128, dtype=np.float32),
        m_norm=col8(inputs[f"L1_attn_{pre}_norm"]),
        m_normf=col8(inputs["L1_attn_normf"]),
        gamma=col8(inputs["L1_ffn_gamma"]),
        beta=col8(inputs["L1_ffn_beta"]),
        alog=alog_dev, dp=dp, convw=convw_dev, convb=convb, dtb_row=dtb_row,
        sel16=sel16.astype(BF), sel8=sel8.astype(BF),
        rwT=rwT, xprojT=xprojT, dtwT=dtwT,
        inprojT=ipb, outprojT=opb, fw1=fw1, fw2=fw2,
    )


def _build_l1(upto=None):
    nc = bass.Bass("TRN2", target_bir_lowering=False)
    io = {}
    io["x1cat"] = nc.dram_tensor("x1cat", [D, LSEQ], F32, kind="ExternalInput").ap()
    io["ident"] = nc.dram_tensor("ident", [128, 128], BF16, kind="ExternalInput").ap()
    io["identf"] = nc.dram_tensor("identf", [128, 128], F32, kind="ExternalInput").ap()
    for nm in ["m_norm", "m_normf", "gamma", "beta"]:
        io[nm] = nc.dram_tensor(nm, [128, 8], F32, kind="ExternalInput").ap()
    io["alog"] = nc.dram_tensor("alog", [128, 256], F32, kind="ExternalInput").ap()
    io["dp"] = nc.dram_tensor("dp", [128, 16], F32, kind="ExternalInput").ap()
    io["convw"] = nc.dram_tensor("convw", [128, 64], F32, kind="ExternalInput").ap()
    io["convb"] = nc.dram_tensor("convb", [128, 16], F32, kind="ExternalInput").ap()
    io["dtb_row"] = nc.dram_tensor("dtb_row", [1, 2048], F32, kind="ExternalInput").ap()
    io["sel16"] = nc.dram_tensor("sel16", [16, 16 * 128], BF16, kind="ExternalInput").ap()
    io["sel8"] = nc.dram_tensor("sel8", [8, 8 * 128], BF16, kind="ExternalInput").ap()
    io["rwT"] = nc.dram_tensor("rwT", [128, 64], F32, kind="ExternalInput").ap()
    io["xprojT"] = nc.dram_tensor("xprojT", [128, 16 * 96], F32, kind="ExternalInput").ap()
    io["dtwT"] = nc.dram_tensor("dtwT", [64, 2048], F32, kind="ExternalInput").ap()
    io["inprojT"] = nc.dram_tensor("inprojT", [32, 128, 1024], F32, kind="ExternalInput").ap()
    io["outprojT"] = nc.dram_tensor("outprojT", [8, 128, 2048], F32, kind="ExternalInput").ap()
    io["fw1"] = nc.dram_tensor("fw1", [8, 16, 128, 8 * 256], F32, kind="ExternalInput").ap()
    io["fw2"] = nc.dram_tensor("fw2", [8, 8, 128, 32 * 128], F32, kind="ExternalInput").ap()
    io["x2p"] = nc.dram_tensor("x2p", [D, TC], BF16, kind="ExternalOutput").ap()
    io["oacc_out"] = None
    io["contrib"] = nc.dram_tensor("contrib", [D, TC], F32, kind="ExternalOutput").ap()
    with ExitStack() as ctx:
        l1 = L1(nc, ctx, io)
        l1.emit(upto=upto)
    return nc


def _build_sum():
    nc = bass.Bass("TRN2", target_bir_lowering=False)
    a = nc.dram_tensor("a", [D, TC], BF16, kind="ExternalInput").ap()
    bb = nc.dram_tensor("bb", [D, TC], F32, kind="ExternalInput").ap()
    o = nc.dram_tensor("o", [D, TC], F32, kind="ExternalOutput").ap()
    with ExitStack() as ctx:
        ent = ctx.enter_context
        sa = ent(nc.sbuf_tensor("sa", [128, TC], BF16))
        sb2 = ent(nc.sbuf_tensor("sb2", [128, TC], F32))
        so = ent(nc.sbuf_tensor("so", [128, TC], F32))
        ld = Sem(ctx, nc, "ld")
        vv = Sem(ctx, nc, "vv")
        st = Sem(ctx, nc, "st")
        blk = ent(nc.Block())

        for k in range(DK):
            @blk.sync
            def _(sy, k=k):
                if k >= 1:
                    sy.wait_ge(vv.h, k)
                ld.inc(sy.dma_start(sa[:, :], a[k * 128:(k + 1) * 128, :]), 16)
                ld.inc(sy.dma_start(sb2[:, :], bb[k * 128:(k + 1) * 128, :]), 16)

            @blk.vector
            def _(v, k=k):
                v.wait_ge(ld.h, 32 * (k + 1))
                if k >= 1:
                    v.wait_ge(st.h, 16 * k)
                vv.inc(v.tensor_tensor(so[:, :], sa[:, :], sb2[:, :], ALU.add))

            @blk.gpsimd
            def _(g, k=k):
                g.wait_ge(vv.h, k + 1)
                st.inc(g.dma_start(o[k * 128:(k + 1) * 128, :], so[:, :]), 16)
                if k == DK - 1:
                    g.wait_ge(st.h, 16 * DK)
    return nc


def dev_run_l1(inputs, x1_parts, upto=None):
    nc = _build_l1(upto=upto)
    in_maps = [_prep_l1_core(inputs, c, x1_parts) for c in range(8)]
    res = run_bass_kernel_spmd(nc, in_maps, core_ids=list(range(8)))
    return res.results


def kernel(**inputs):
    ncA = _build_l0()
    in_a = [_prep_l0_core(inputs, c) for c in range(8)]
    resA = run_bass_kernel_spmd(ncA, in_a, core_ids=list(range(8)))
    x1 = [np.asarray(r["x1"]) for r in resA.results]

    ncB = _build_l1()
    in_b = [_prep_l1_core(inputs, c, x1) for c in range(8)]
    resB = run_bass_kernel_spmd(ncB, in_b, core_ids=list(range(8)))
    x2p = [np.asarray(r["x2p"]) for r in resB.results]
    contrib = [np.asarray(r["contrib"]) for r in resB.results]

    ncC = _build_sum()
    in_c = [dict(a=x2p[c], bb=contrib[c ^ 4]) for c in range(8)]
    resC = run_bass_kernel_spmd(ncC, in_c, core_ids=list(range(8)))
    x2 = [np.asarray(r["o"]) for r in resC.results]

    out = np.empty((4, LSEQ, D), np.float32)
    for c in range(8):
        b, g = c & 3, c >> 2
        if g == 0:
            out[b, 0:TC, :] = x2[c].T
        else:
            out[b, TC:LSEQ, :] = x2[c][:, ::-1].T
    return out

